# version 23
# baseline (speedup 1.0000x reference)
import contextlib
import numpy as np
import ml_dtypes
import concourse.bass as bass
import concourse.mybir as mybir
from concourse.bass_utils import run_bass_kernel_spmd

F32 = mybir.dt.float32
BF16 = mybir.dt.bfloat16
I32 = mybir.dt.int32
AF = mybir.ActivationFunctionType
ALU = mybir.AluOpType
AX = mybir.AxisListType

B = 16
NCORE = 8
B2 = B // NCORE
S = 2048
D = 2048
NC = D // 128
NT = S // 128
H = 16
HD = 128
SCALE = HD ** -0.5
NE = 8
FI = 8192
NI = FI // 128
EPS = 1e-5
BF = ml_dtypes.bfloat16


class Ctr:
    def __init__(self, stack, nc, name):
        self.sem = stack.enter_context(nc.semaphore(name))
        self.v = 0

    def inc(self, instr, n=1):
        instr.then_inc(self.sem, n)
        self.v += n
        return self.v


def emit_l0_ln(ctx, x, gw, ident, xnT, gates, wprep_args=None):
    nc = ctx.nc
    PFX = 'a_'
    with contextlib.ExitStack() as ex:
        E = ex.enter_context
        xt = [E(nc.sbuf_tensor(PFX + f"xt{p}", [128, D], F32)) for p in range(2)]
        xnf = [E(nc.sbuf_tensor(PFX + f"xnf{p}", [128, D], F32)) for p in range(2)]
        xnb = E(nc.sbuf_tensor(PFX + "xnb", [128, D], BF16))
        xnl = E(nc.sbuf_tensor(PFX + "xnl", [128, D], BF16))
        xnT_sb = E(nc.sbuf_tensor(PFX + "xnT_sb", [128, NC * S], BF16))
        xnloT = E(nc.sbuf_tensor(PFX + "xnloT", [128, NC * 128], BF16))
        id_sb = E(nc.sbuf_tensor(PFX + "id_sb", [128, 128], BF16))
        gw_sb = E(nc.sbuf_tensor(PFX + "gw_sb", [NE, D], F32))
        gwh = E(nc.sbuf_tensor(PFX + "gwh", [NE, D], BF16))
        gwl = E(nc.sbuf_tensor(PFX + "gwl", [NE, D], BF16))
        gwT = E(nc.sbuf_tensor(PFX + "gwT", [128, 2 * NC * NE], BF16))
        mv = E(nc.sbuf_tensor(PFX + "mv", [128, 8], F32))
        lnv = E(nc.sbuf_tensor(PFX + "lnv", [128, 2], F32))
        rstd = E(nc.sbuf_tensor(PFX + "rstd", [128, 2], F32))
        logit = E(nc.sbuf_tensor(PFX + "logit", [128, NT * NE], F32))
        gat = E(nc.sbuf_tensor(PFX + "gat", [128, NT * NE], F32))
        sc1 = E(nc.sbuf_tensor(PFX + "sc1", [128, NT * NE], F32))
        sc2 = E(nc.sbuf_tensor(PFX + "sc2", [128, NT * NE], F32))
        sc3 = E(nc.sbuf_tensor(PFX + "sc3", [128, NT * NE], F32))
        sm = E(nc.sbuf_tensor(PFX + "sm", [128, 5 * NT], F32))
        pb = [E(nc.psum_tensor(PFX + f"pb{k}", [128, 1024], BF16)) for k in range(4)]
        pL = E(nc.psum_tensor(PFX + "pL", [128, 512], F32))
        cL = ctx.cL
        cV = ctx.cV
        cA = ctx.cA
        cP = ctx.cP
        cS = ctx.cS
        block = ctx.block

        @block.sync
        def _(sy):
            sy.dma_start(out=id_sb[:], in_=ident[:, :]).then_inc(cL.sem, 16)
            sy.dma_start(out=gw_sb[:], in_=gw[:, :]).then_inc(cL.sem, 16)
        cL.v += 32
        ld_prep = cL.v
        if wprep_args is not None:
            wprep_step = make_wprep_stepper(ctx, E, id_sb, ld_prep, *wprep_args)
        else:
            wprep_step = lambda k: None

        cA.v += 1
        a_gwh = cA.v
        @block.scalar
        def _(sc):
            sc.wait_ge(cL.sem, ld_prep)
            sc.copy(gwh[:], gw_sb[:]).then_inc(cA.sem, 1)

        @block.vector
        def _(v):
            v.wait_ge(cA.sem, a_gwh)
            cV.inc(v.tensor_tensor(gwl[:], gw_sb[:], gwh[:], op=ALU.subtract))
        v_gwl = cV.v

        @block.tensor
        def _(t):
            t.wait_ge(cV.sem, v_gwl)
            for dc in range(NC):
                t.transpose(pb[0][:, dc * 8:(dc + 1) * 8], gwh[0:NE, dc * 128:(dc + 1) * 128],
                            id_sb[0:NE, 0:NE])
                last = t.transpose(pb[0][:, 128 + dc * 8:128 + (dc + 1) * 8],
                                   gwl[0:NE, dc * 128:(dc + 1) * 128], id_sb[0:NE, 0:NE])
            cP.inc(last)
        p_gwT = cP.v

        @block.vector
        def _(v):
            v.wait_ge(cP.sem, p_gwT)
            cV.inc(v.tensor_copy(gwT[:, 0:256], pb[0][:, 0:256]))
        v_gwT = cV.v

        evac_done = {}
        logit_done = {}
        pv_xn2 = {0: 0, 1: 0}
        for b in range(B2):
            for i in range(NT):
                it = b * NT + i
                par = it % 2
                @block.sync
                def _(sy, b=b, i=i, par=par, it=it):
                    if it >= 2:
                        sy.wait_ge(cV.sem, evac_done[(b, i - 2) if i >= 2 else (b - 1, NT + i - 2)])
                    sy.dma_start(out=xt[par][:], in_=x[b, i * 128:(i + 1) * 128, :]).then_inc(cL.sem, 16)
                cL.v += 16
                ld_here = cL.v

                cA.v += 1
                a_st = cA.v
                @block.scalar
                def _(sc, par=par, ld=ld_here, pvx=pv_xn2[par]):
                    sc.wait_ge(cL.sem, ld)
                    if pvx:
                        sc.wait_ge(cV.sem, pvx)
                    sc.activation(xnf[par][:], xt[par][:], AF.Identity,
                                  accum_out=mv[:, par * 4: par * 4 + 1])
                    sc.activation(xnf[par][:], xt[par][:], AF.Square,
                                  accum_out=mv[:, par * 4 + 3: par * 4 + 4]).then_inc(cA.sem, 1)

                @block.vector
                def _(v, a_st=a_st, o=par * 4):
                    v.wait_ge(cA.sem, a_st)
                    v.tensor_scalar(mv[:, o:o + 1], mv[:, o:o + 1], 1.0 / D, None, op0=ALU.mult)
                    cV.inc(v.tensor_scalar(mv[:, o + 3:o + 4], mv[:, o + 3:o + 4], 1.0 / D, EPS,
                                           op0=ALU.mult, op1=ALU.add))
                    v.wait_ge(cV.sem, cV.v)
                    cV.inc(v.tensor_tensor(mv[:, o + 2:o + 3], mv[:, o:o + 1], mv[:, o:o + 1],
                                           op=ALU.mult))
                    v.wait_ge(cV.sem, cV.v)
                    cV.inc(v.tensor_tensor(mv[:, o + 1:o + 2], mv[:, o + 3:o + 4],
                                           mv[:, o + 2:o + 3], op=ALU.subtract))
                v_bn = cV.v

                cA.v += 2
                a_rstd = cA.v
                @block.scalar
                def _(sc, v_bn=v_bn, a1=a_rstd - 1, par=par, o=par * 4):
                    sc.wait_ge(cV.sem, v_bn)
                    sc.activation(lnv[:, par:par + 1], mv[:, o + 1:o + 2], AF.Ln).then_inc(cA.sem, 1)
                    sc.wait_ge(cA.sem, a1)
                    sc.activation(rstd[:, par:par + 1], lnv[:, par:par + 1], AF.Exp,
                                  scale=-0.5).then_inc(cA.sem, 1)

                @block.vector
                def _(v, par=par, a_rstd=a_rstd, o=par * 4):
                    v.wait_ge(cA.sem, a_rstd)
                    v.tensor_scalar(xnf[par][:], xt[par][:], mv[:, o:o + 1],
                                    rstd[:, par:par + 1], op0=ALU.subtract, op1=ALU.mult)
                    v.tensor_copy(xnb[:], xnf[par][:])
                    cV.inc(v.tensor_tensor(xnl[:], xnf[par][:], xnb[:], op=ALU.subtract))
                v_xn = cV.v
                pv_xn2[par] = v_xn

                @block.tensor
                def _(t, v_xn=v_xn, it=it):
                    t.wait_ge(cV.sem, v_xn)
                    if it > 0:
                        t.wait_ge(cV.sem, evac_done[prev_bi])
                    else:
                        t.wait_ge(cV.sem, v_gwT)
                    for dc in range(NC):
                        bank, sl = divmod(dc, 8)
                        t.transpose(pb[bank][:, sl * 128:(sl + 1) * 128],
                                    xnb[:, dc * 128:(dc + 1) * 128], id_sb[:])
                        last = t.transpose(pb[2 + bank][:, sl * 128:(sl + 1) * 128],
                                           xnl[:, dc * 128:(dc + 1) * 128], id_sb[:])
                    cP.inc(last)
                p_tr = cP.v

                @block.vector
                def _(v, i=i, p_tr=p_tr, it=it):
                    v.wait_ge(cP.sem, p_tr)
                    if it > 0:
                        v.wait_ge(cP.sem, logit_done[prev_bi])
                    for bank in range(2):
                        v.tensor_copy(
                            xnT_sb[:].rearrange("p (c t) -> p c t", c=NC)[:, bank * 8:(bank + 1) * 8,
                                                                          i * 128:(i + 1) * 128],
                            pb[bank][:].rearrange("p (c t) -> p c t", c=8))
                    for bank in range(2):
                        last = v.tensor_copy(
                            xnloT[:].rearrange("p (c t) -> p c t", c=NC)[:, bank * 8:(bank + 1) * 8, :],
                            pb[2 + bank][:].rearrange("p (c t) -> p c t", c=8))
                    cV.inc(last)
                evac_done[(b, i)] = cV.v

                @block.tensor
                def _(t, b=b, i=i, ev=evac_done[(b, i)]):
                    t.wait_ge(cV.sem, ev)
                    t.wait_ge(cV.sem, v_gwT)
                    n = 0
                    for dc in range(NC):
                        hi_t = xnT_sb[:, dc * S + i * 128: dc * S + (i + 1) * 128]
                        lo_t = xnloT[:, dc * 128:(dc + 1) * 128]
                        for lhs, rhs in ((hi_t, gwT[:, dc * 8:(dc + 1) * 8]),
                                         (lo_t, gwT[:, dc * 8:(dc + 1) * 8]),
                                         (hi_t, gwT[:, 128 + dc * 8:128 + (dc + 1) * 8])):
                            last = t.matmul(pL[:, 0:8], lhs, rhs, start=(n == 0), stop=(n == 3 * NC - 1))
                            n += 1
                    cP.inc(last)
                logit_done[(b, i)] = cP.v

                @block.vector
                def _(v, b=b, i=i, pl=logit_done[(b, i)]):
                    v.wait_ge(cP.sem, pl)
                    cV.inc(v.tensor_copy(logit[:, i * 8:(i + 1) * 8], pL[:, 0:8]))
                prev_bi = (b, i)
                wprep_step(5)

            def c3(t):
                return t[:].rearrange("p (c e) -> p c e", e=NE)

            def b3(t, off):
                return (t[:, off:off + NT].rearrange("p (c e) -> p c e", e=1)
                        .broadcast_to([128, NT, NE]))

            m1s, m2s, dds, w1s, w2s = 0, NT, 2 * NT, 3 * NT, 4 * NT

            @block.vector
            def _(v):
                cV.inc(v.tensor_reduce(sm[:, m1s:m1s + NT], c3(logit), axis=AX.X, op=ALU.max))
                v.wait_ge(cV.sem, cV.v)
                v.tensor_tensor(c3(sc1), c3(logit), b3(sm, m1s), op=ALU.is_ge)
                cV.inc(v.scalar_tensor_tensor(c3(sc3), c3(sc1), -1e30, c3(logit),
                                              op0=ALU.mult, op1=ALU.add))
                v.wait_ge(cV.sem, cV.v)
                cV.inc(v.tensor_reduce(sm[:, m2s:m2s + NT], c3(sc3), axis=AX.X, op=ALU.max))
                v.wait_ge(cV.sem, cV.v)
                v.tensor_tensor(c3(sc2), c3(sc3), b3(sm, m2s), op=ALU.is_ge)
                cV.inc(v.tensor_tensor(sm[:, dds:dds + NT], sm[:, m2s:m2s + NT],
                                       sm[:, m1s:m1s + NT], op=ALU.subtract))
            v_dd = cV.v

            cA.v += 1
            a_e = cA.v
            @block.scalar
            def _(sc, v_dd=v_dd):
                sc.wait_ge(cV.sem, v_dd)
                sc.activation(sm[:, dds:dds + NT], sm[:, dds:dds + NT], AF.Exp).then_inc(cA.sem, 1)

            @block.vector
            def _(v, a_e=a_e):
                v.wait_ge(cA.sem, a_e)
                cV.inc(v.tensor_scalar(sm[:, dds:dds + NT], sm[:, dds:dds + NT], 1.0, None, op0=ALU.add))
                v.wait_ge(cV.sem, cV.v)
                cV.inc(v.reciprocal(sm[:, w1s:w1s + NT], sm[:, dds:dds + NT]))
                v.wait_ge(cV.sem, cV.v)
                cV.inc(v.tensor_scalar(sm[:, w2s:w2s + NT], sm[:, w1s:w1s + NT], -1.0, 1.0,
                                       op0=ALU.mult, op1=ALU.add))
                v.wait_ge(cV.sem, cV.v)
                cV.inc(v.tensor_tensor(c3(gat), c3(sc1), b3(sm, w1s), op=ALU.mult))
                v.wait_ge(cV.sem, cV.v)
                cV.inc(v.tensor_tensor(c3(sc2), c3(sc2), b3(sm, w2s), op=ALU.mult))
                v.wait_ge(cV.sem, cV.v)
                cV.inc(v.tensor_tensor(gat[:], gat[:], sc2[:], op=ALU.add))
            v_gates = cV.v


            @block.sync
            def _(sy, b=b, v_gates=v_gates, ev=evac_done[(b, NT - 1)]):
                sy.wait_ge(cV.sem, max(v_gates, ev))
                for dc in range(NC):
                    sy.dma_start(out=xnT[b, dc * 128:(dc + 1) * 128, :],
                                 in_=xnT_sb[:, dc * S:(dc + 1) * S]).then_inc(cS.sem, 16)
                sy.dma_start(out=gates[b, :, :], in_=gat[:, :]).then_inc(cS.sem, 16)
            cS.v += 17 * 16
            st_b = cS.v
            if b + 1 < B2:
                @block.tensor
                def _(t, st_b=st_b):
                    t.wait_ge(cS.sem, st_b)
                @block.vector
                def _(v, st_b=st_b):
                    v.wait_ge(cS.sem, st_b)

        @block.sync
        def _(sy):
            sy.wait_ge(cS.sem, cS.v)


def emit_l0_attn(ctx, xnT_h, wq_h, wk_h, wv_h, ident, identf, attnT):
    nc = ctx.nc
    PFX = 'b_'
    with contextlib.ExitStack() as ex:
        E = ex.enter_context
        xnb_sb = E(nc.sbuf_tensor(PFX + "xnb_sb", [128, NC * S], BF16))
        id_sb = E(nc.sbuf_tensor(PFX + "id_sb", [128, 128], BF16))
        id32_sb = E(nc.sbuf_tensor(PFX + "id32_sb", [128, 128], F32))
        onesc = E(nc.sbuf_tensor(PFX + "onesc", [128, 1], BF16))
        onesr = E(nc.sbuf_tensor(PFX + "onesr", [1, 128], BF16))
        w_sb = [[E(nc.sbuf_tensor(PFX + f"w_sb{p}{j}", [128, D], BF16)) for j in range(3)] for p in range(2)]
        wT = [E(nc.sbuf_tensor(PFX + f"wT{j}", [128, NC * 128], BF16)) for j in range(3)]
        qT = E(nc.sbuf_tensor(PFX + "qT", [128, S], BF16))
        kT = E(nc.sbuf_tensor(PFX + "kT", [128, S], BF16))
        vT = E(nc.sbuf_tensor(PFX + "vT", [128, S], BF16))
        vtok = E(nc.sbuf_tensor(PFX + "vtok", [128, NT * 128], BF16))
        pT_sb = E(nc.sbuf_tensor(PFX + "pT_sb", [128, NT * 512], BF16))
        row_sb = [E(nc.sbuf_tensor(PFX + f"row_sb{p}", [128, S], BF16)) for p in range(2)]
        mx = E(nc.sbuf_tensor(PFX + "mx", [128, 3], F32))
        ngrow = [E(nc.sbuf_tensor(PFX + f"ngrow{p}", [1, 512], BF16)) for p in range(2)]
        ngmin = E(nc.sbuf_tensor(PFX + "ngmin", [1, 2], BF16))
        negC = E(nc.sbuf_tensor(PFX + "negC", [128, 2], F32))
        onesq = E(nc.sbuf_tensor(PFX + "onesq", [128, 128], BF16))
        Rf = E(nc.sbuf_tensor(PFX + "Rf", [128, 512], F32))
        pA = E(nc.psum_tensor(PFX + "pA", [128, 1024], F32))
        pST = [E(nc.psum_tensor(PFX + f"pST{k}", [128, 512], F32)) for k in range(3)]
        pSum = E(nc.psum_tensor(PFX + "pSum", [128, 512], F32))
        pO = E(nc.psum_tensor(PFX + "pO", [128, 512], F32))
        pNg = E(nc.psum_tensor(PFX + "pNg", [128, 512], F32))
        cL, cW, cV, cA, cP, cS = ctx.cL, ctx.cW, ctx.cV, ctx.cA, ctx.cP, ctx.cS
        block = ctx.block

        @block.sync
        def _(sy):
            sy.dma_start(out=id_sb[:], in_=ident[:, :]).then_inc(cL.sem, 16)
            sy.dma_start(out=id32_sb[:], in_=identf[:, :]).then_inc(cL.sem, 16)
        cL.v += 32

        @block.vector
        def _(v):
            v.memset(onesc[:], 1.0)
            v.memset(onesr[:], 1.0)
            cV.inc(v.memset(onesq[:], 1.0))
        ones_ready = cV.v

        wprep_done = {}
        row_store = {}
        def issue_wload(hh):
            par = hh % 2
            h = hh % H
            @block.gpsimd
            def _(g, par=par, h=h, hh=hh):
                if hh >= 2:
                    g.wait_ge(cP.sem, wprep_done[hh - 2])
                for j, w in enumerate((wq_h, wk_h, wv_h)):
                    g.dma_start(out=w_sb[par][j][:],
                                in_=w[h * 128:(h + 1) * 128, :]).then_inc(cW.sem, 16)
            cW.v += 48
            return cW.v

        wload_done = {0: issue_wload(0), 1: issue_wload(1)}
        pj_busy = [None]
        prev_hd = [0, 0, 0]

        for b in range(B2):
            @block.sync
            def _(sy, b=b, xn_free=cP.v):
                if b > 0:
                    sy.wait_ge(cP.sem, xn_free)
                for dc in range(NC):
                    sy.dma_start(out=xnb_sb[:, dc * S:(dc + 1) * S],
                                 in_=xnT_h[b, dc * 128:(dc + 1) * 128, :]).then_inc(cL.sem, 16)
            cL.v += 16 * NC
            ld_xn = cL.v

            for h in range(H):
                hh = b * H + h
                par = hh % 2

                pa_bf = pA[:].bitcast(BF16)
                wT_ev = 0
                for j in range(3):
                    @block.tensor
                    def _(t, j=j, hh=hh, ph=prev_hd[0], pe=wT_ev):
                        if j == 0:
                            t.wait_ge(cW.sem, wload_done[hh])
                            if ph:
                                t.wait_ge(cV.sem, ph)
                        else:
                            t.wait_ge(cV.sem, pe)
                        for dc in range(NC):
                            last = t.transpose(pa_bf[:, dc * 128:(dc + 1) * 128],
                                               w_sb[par][j][0:128, dc * 128:(dc + 1) * 128],
                                               id_sb[:])
                        cP.inc(last)
                    tr = cP.v

                    @block.vector
                    def _(v, j=j, tr=tr):
                        v.wait_ge(cP.sem, tr)
                        cV.inc(v.tensor_copy(
                            wT[j][:].rearrange("p (c t) -> p c t", t=128),
                            pA[:].bitcast(BF16).rearrange("p (c t) -> p c t", t=128)))
                    wT_ev = cV.v
                wprep_done[hh] = cP.v
                wT_ready = cV.v
                if hh + 2 < B2 * H:
                    wload_done[hh + 2] = issue_wload(hh + 2)

                pjb = [pA[:, 0:512], pA[:, 512:1024], pST[0][:], pST[1][:]]
                for j, dstT in enumerate((qT, kT, vT)):
                    @block.tensor
                    def _(t, j=j, wT_ready=wT_ready, ld_xn=ld_xn, pj=pj_busy[0]):
                        t.wait_ge(cV.sem, wT_ready)
                        t.wait_ge(cL.sem, ld_xn)
                        if pj is not None:
                            t.wait_ge(pj[0].sem, pj[1])
                        for dc in range(NC):
                            for n in range(4):
                                last = t.matmul(pjb[n], wT[j][:, dc * 128:(dc + 1) * 128],
                                                xnb_sb[:, dc * S + n * 512: dc * S + n * 512 + 512],
                                                start=(dc == 0), stop=(dc == NC - 1))
                        cP.inc(last)
                    pj_mm = cP.v

                    @block.vector
                    def _(v, j=j, dstT=dstT, pj_mm=pj_mm):
                        v.wait_ge(cP.sem, pj_mm)
                        for n in range(4):
                            if j == 0:
                                last = v.tensor_scalar(dstT[:, n * 512:(n + 1) * 512], pjb[n],
                                                       float(SCALE), None, op0=ALU.mult)
                            else:
                                last = v.tensor_copy(dstT[:, n * 512:(n + 1) * 512], pjb[n])
                        cV.inc(last)
                    pj_busy[0] = (cV, cV.v)
                proj_done_v = cV.v

                @block.tensor
                def _(t, proj_done_v=proj_done_v):
                    t.wait_ge(cV.sem, proj_done_v)
                    for tc in range(NT):
                        last = t.transpose(pA[:].bitcast(BF16)[:, tc * 128:(tc + 1) * 128],
                                           vT[0:128, tc * 128:(tc + 1) * 128], id_sb[:])
                    cP.inc(last)
                vt_tr = cP.v

                @block.vector
                def _(v, vt_tr=vt_tr):
                    v.wait_ge(cP.sem, vt_tr)
                    cV.inc(v.tensor_copy(
                        vtok[:].rearrange("p (c t) -> p c t", t=128),
                        pA[:].bitcast(BF16).rearrange("p (c t) -> p c t", t=128)))
                vtok_ready = cV.v

                max_hist = {}
                ng_ev = {}
                exp_hist = {}
                p2_last = {}
                p2_tt = {}
                ngtr_hist = {}
                negc_ev = {}

                red_hist = {}

                def emit_pass1_half(qblk, sub, half):
                    qi = qblk * 4 + sub
                    @block.tensor
                    def _(t, qi=qi, half=half,
                          mh=(max_hist.get(qi - 1, 0) if half == 0 else red_hist[qi]),
                          vr=vtok_ready):
                        if half == 0 and qi % 4 == 0:
                            t.wait_ge(cV.sem, vr)
                        if mh:
                            t.wait_ge(cV.sem, mh)
                        for n in range(2):
                            last = t.matmul(pA[:, n * 512:(n + 1) * 512],
                                            qT[:, qi * 128:(qi + 1) * 128],
                                            kT[:, half * 1024 + n * 512: half * 1024 + n * 512 + 512],
                                            start=True, stop=True)
                        cP.inc(last)
                    s_done = cP.v

                    @block.vector
                    def _(v, half=half, s_done=s_done):
                        v.wait_ge(cP.sem, s_done)
                        cV.inc(v.tensor_reduce(mx[:, half:half + 1], pA[:], axis=AX.X,
                                               op=ALU.max, negate=True))
                    red_hist[qi] = cV.v
                    if half == 1:
                        @block.vector
                        def _(v, rv=cV.v):
                            v.wait_ge(cV.sem, rv)
                            cV.inc(v.tensor_tensor(mx[:, 2:3], mx[:, 0:1], mx[:, 1:2],
                                                   op=ALU.min))
                            v.wait_ge(cV.sem, cV.v)
                        max_hist[qi] = cV.v

                def emit_ngtr(qblk, sub):
                    qi = qblk * 4 + sub
                    @block.tensor
                    def _(t, qi=qi, sub=sub, mh=max_hist[qi],
                          ngv=negc_ev.get(qblk - 1, 0)):
                        t.wait_ge(cV.sem, mh)
                        if sub == 0 and ngv:
                            t.wait_ge(cV.sem, ngv)
                        cP.inc(t.transpose(pNg[0:1, sub * 128:(sub + 1) * 128],
                                           mx[:, 2:3], id32_sb[:]))
                    ngtr_hist[qblk] = cP.v

                def emit_ng_evac(qblk):
                    @block.vector
                    def _(v, qblk=qblk, ngt=ngtr_hist[qblk], par2=qblk % 2,
                          p2=p2_last.get(qblk - 2, 0)):
                        v.wait_ge(cP.sem, ngt)
                        if p2:
                            v.wait_ge(cP.sem, p2)
                        cV.inc(v.tensor_copy(ngrow[par2][:], pNg[0:1, 0:512]))
                        v.wait_ge(cV.sem, cV.v)
                        cV.inc(v.tensor_reduce(ngmin[0:1, par2:par2 + 1], ngrow[par2][:],
                                               axis=AX.X, op=ALU.min))
                        v.wait_ge(cV.sem, cV.v)
                    ng_ev[qblk] = cV.v

                    @block.tensor
                    def _(t, qblk=qblk, ngv=ng_ev[qblk], par2=qblk % 2,
                          pe=negc_ev.get(qblk - 1, 0)):
                        t.wait_ge(cV.sem, ngv)
                        if pe:
                            t.wait_ge(cV.sem, pe)
                        cP.inc(t.matmul(pNg[:, 0:1], onesr[0:1, :], ngmin[0:1, par2:par2 + 1],
                                        start=True, stop=True))
                    bc_mm = cP.v

                    @block.vector
                    def _(v, qblk=qblk, bc_mm=bc_mm, par2=qblk % 2):
                        v.wait_ge(cP.sem, bc_mm)
                        cV.inc(v.tensor_copy(negC[:, par2:par2 + 1], pNg[:, 0:1]))
                    negc_ev[qblk] = cV.v

                def emit_pass2_kcs(qblk, kcs):
                    par2 = qblk % 2
                    for kc in kcs:
                        stp = kc % 3
                        @block.tensor
                        def _(t, qblk=qblk, kc=kc, stp=stp, par2=par2,
                              ngv=ng_ev[qblk], ex=exp_hist.get((qblk, kc - 3), 0),
                              px=(p2_last.get(qblk - 1, 0) if kc == 0 else 0),
                              ones_ready=ones_ready):
                            if kc == 0:
                                t.wait_ge(cV.sem, max(ngv, ones_ready))
                                if px:
                                    t.wait_ge(cP.sem, px)
                            if ex:
                                t.wait_ge(cA.sem, ex)
                            last = t.matmul(pST[stp][:], kT[:, kc * 128:(kc + 1) * 128],
                                            qT[:, qblk * 512:(qblk + 1) * 512],
                                            start=True, stop=True)
                            cP.inc(last)
                        st_mm = cP.v

                        cA.v += 1
                        exp_hist[(qblk, kc)] = cA.v
                        @block.scalar
                        def _(sc, qblk=qblk, kc=kc, stp=stp, st_mm=st_mm, par2=par2,
                              ngc=negc_ev[qblk]):
                            sc.wait_ge(cP.sem, st_mm)
                            if kc == 0:
                                sc.wait_ge(cV.sem, ngc)
                            sc.activation(pT_sb[:, kc * 512:(kc + 1) * 512], pST[stp][:],
                                          AF.Exp, bias=negC[:, par2:par2 + 1]).then_inc(cA.sem, 1)

                        if kc >= 1:
                            emit_sumpv(qblk, kc - 1)
                    if kcs[-1] == NT - 1:
                        emit_sumpv(qblk, NT - 1)
                        p2_last[qblk] = cP.v

                def emit_sumpv(qblk, kc):
                    @block.tensor
                    def _(t, qblk=qblk, kc=kc, ex=exp_hist[(qblk, kc)],
                          pv=(p2_tt.get(qblk - 1, 0) if kc == 0 else 0)):
                        t.wait_ge(cA.sem, ex)
                        if pv:
                            t.wait_ge(cV.sem, pv)
                        t.matmul(pSum[:], onesq[:, :],
                                 pT_sb[:, kc * 512:(kc + 1) * 512],
                                 start=(kc == 0), stop=(kc == NT - 1))
                        last = t.matmul(pO[:], vtok[:, kc * 128:(kc + 1) * 128],
                                        pT_sb[:, kc * 512:(kc + 1) * 512],
                                        start=(kc == 0), stop=(kc == NT - 1))
                        cP.inc(last)

                def emit_pass2_tail(qblk):
                    @block.vector
                    def _(v, qblk=qblk, pl=p2_last[qblk], par=par, rs=row_store.get(par, 0)):
                        v.wait_ge(cP.sem, pl)
                        if qblk == 0 and rs:
                            v.wait_ge(cS.sem, rs)
                        cV.inc(v.reciprocal(Rf[:], pSum[:]))
                        v.wait_ge(cV.sem, cV.v)
                        cV.inc(v.tensor_tensor(row_sb[par][:, qblk * 512:(qblk + 1) * 512],
                                               pO[:], Rf[:], op=ALU.mult))
                    p2_tt[qblk] = cV.v

                for qblk in range(5):
                    for sub in range(4):
                        if qblk < 4:
                            emit_pass1_half(qblk, sub, 0)
                        if qblk >= 1:
                            emit_pass2_kcs(qblk - 1, [sub * 4, sub * 4 + 1])
                        if qblk < 4:
                            emit_pass1_half(qblk, sub, 1)
                        if qblk >= 1:
                            emit_pass2_kcs(qblk - 1, [sub * 4 + 2, sub * 4 + 3])
                        if qblk < 4:
                            emit_ngtr(qblk, sub)
                    if qblk < 4:
                        emit_ng_evac(qblk)
                    if qblk >= 1:
                        emit_pass2_tail(qblk - 1)

                prev_hd[0] = max_hist[NT - 1]

                @block.gpsimd
                def _(gq, b=b, h=h, par=par, oe=p2_tt[3]):
                    gq.wait_ge(cV.sem, oe)
                    gq.dma_start(out=attnT[b, h * 128:(h + 1) * 128, :],
                                 in_=row_sb[par][:]).then_inc(cS.sem, 16)
                cS.v += 16
                row_store[par] = cS.v

        @block.sync
        def _(sy):
            sy.wait_ge(cS.sem, cS.v)

def emit_l0_wo(ctx, attnT_h, wo_h, ident, sT):
    nc = ctx.nc
    PFX = 'c_'
    with contextlib.ExitStack() as ex:
        E = ex.enter_context
        id_sb = E(nc.sbuf_tensor(PFX + "id_sb", [128, 128], BF16))
        wo_sb = [E(nc.sbuf_tensor(PFX + f"wo_sb{p}", [128, D], BF16)) for p in range(2)]
        wo_f = [E(nc.sbuf_tensor(PFX + f"wo_f{p}", [128, D], F32)) for p in range(2)]
        woT = E(nc.sbuf_tensor(PFX + "woT", [128, NC * D], BF16))
        at_sb = [E(nc.sbuf_tensor(PFX + f"at_sb{p}", [128, NC * 512], BF16)) for p in range(2)]
        st_sb = [E(nc.sbuf_tensor(PFX + f"st_sb{p}", [128, 512], BF16)) for p in range(2)]
        st2_sb = [E(nc.sbuf_tensor(PFX + f"st2_sb{p}", [128, 512], BF16)) for p in range(2)]
        pT0 = E(nc.psum_tensor(PFX + "pT0", [128, 1024], BF16))
        pT1 = E(nc.psum_tensor(PFX + "pT1", [128, 1024], BF16))
        pPJ = [E(nc.psum_tensor(PFX + f"pPJ{p}", [128, 512], F32)) for p in range(2)]
        cL = ctx.cL
        cW = ctx.cW
        cV = ctx.cV
        cP = ctx.cP
        cS = ctx.cS
        block = ctx.block

        @block.sync
        def _(sy):
            sy.dma_start(out=id_sb[:], in_=ident[:, :]).then_inc(cL.sem, 16)
        cL.v += 16

        pT = [pT0, pT1]
        woT_done = 0
        conv_done = {0: 0, 1: 0}
        for oc in range(NC):
            par = oc % 2
            @block.sync
            def _(sy, oc=oc, par=par, cv=conv_done[par]):
                if oc >= 2:
                    sy.wait_ge(cV.sem, cv)
                sy.dma_start(out=wo_f[par][:],
                             in_=wo_h[oc * 128:(oc + 1) * 128, :]).then_inc(cW.sem, 16)
            cW.v += 16
            wl_raw = cW.v

            @block.vector
            def _(v, oc=oc, par=par, wl_raw=wl_raw, prev=woT_done):
                v.wait_ge(cW.sem, wl_raw)
                if oc >= 2:
                    v.wait_ge(cP.sem, prev)
                cV.inc(v.tensor_copy(wo_sb[par][:], wo_f[par][:]))
            conv_done[par] = cV.v
            wl = (cV, cV.v)
            for half in range(2):
                bk = half
                @block.tensor
                def _(t, oc=oc, par=par, half=half, bk=bk, wl=wl):
                    t.wait_ge(wl[0].sem, wl[1])
                    if oc >= 1:
                        t.wait_ge(cV.sem, ev_done[bk])
                    elif half == 0:
                        t.wait_ge(cL.sem, 16)
                    for s_idx in range(8):
                        dc = half * 8 + s_idx
                        last = t.transpose(pT[bk][:, s_idx * 128:(s_idx + 1) * 128],
                                           wo_sb[par][0:128, dc * 128:(dc + 1) * 128], id_sb[:])
                    cP.inc(last)
                tr = cP.v

                @block.vector
                def _(v, oc=oc, half=half, bk=bk, tr=tr):
                    v.wait_ge(cP.sem, tr)
                    cV.inc(v.tensor_copy(
                        woT[:].rearrange("p (c t) -> p c t", t=D)[:, half * 8:half * 8 + 8,
                                                                  oc * 128:(oc + 1) * 128],
                        pT[bk][:].rearrange("p (c t) -> p c t", t=128)))
                if oc == 0 and half == 0:
                    ev_done = {0: 0, 1: 0}
                ev_done[bk] = cV.v
            woT_done = cP.v
        woT_ready = cV.v

        st_free = {0: 0, 1: 0}
        tr_free = {0: 0, 1: 0}
        at_free = {0: 0, 1: 0}
        pj_free = {0: 0, 1: 0}
        pj_free_eng = {0: None, 1: None}
        for b in range(B2):
            for n in range(4):
                nb = (b * 4 + n) % 2
                @block.sync
                def _(sy, b=b, n=n, nb=nb):
                    if at_free[nb]:
                        sy.wait_ge(cP.sem, at_free[nb])
                    for dc in range(NC):
                        sy.dma_start(out=at_sb[nb][:, dc * 512:(dc + 1) * 512],
                                     in_=attnT_h[b, dc * 128:(dc + 1) * 128,
                                                 n * 512:(n + 1) * 512]).then_inc(cL.sem, 16)
                cL.v += 16 * NC
                at_ld = cL.v

                for oc in range(NC):
                    pj = oc % 2
                    @block.tensor
                    def _(t, oc=oc, nb=nb, pj=pj, at_ld=at_ld, wr=woT_ready):
                        t.wait_ge(cL.sem, at_ld)
                        t.wait_ge(cV.sem, wr)
                        if pj_free_eng[pj] is not None:
                            eng, val = pj_free_eng[pj]
                            t.wait_ge(eng.sem, val)
                        for dc in range(NC):
                            last = t.matmul(pPJ[pj][:], woT[:, dc * D + oc * 128: dc * D + (oc + 1) * 128],
                                            at_sb[nb][:, dc * 512:(dc + 1) * 512],
                                            start=(dc == 0), stop=(dc == NC - 1))
                        cP.inc(last)
                    mm_done = cP.v
                    if oc == NC - 1:
                        at_free[nb] = mm_done

                    sp = oc % 2
                    @block.vector
                    def _(v, sp=sp, pj=pj, mm_done=mm_done, tfree=tr_free[oc % 2]):
                        v.wait_ge(cP.sem, mm_done)
                        if tfree:
                            v.wait_ge(cP.sem, tfree)
                        cV.inc(v.tensor_copy(st_sb[sp][:], pPJ[pj][:]))
                    pj_free_eng[pj] = (cV, cV.v)
                    ev = cV.v

                    bk = oc % 2
                    @block.tensor
                    def _(t, sp=sp, bk=bk, ev=ev, efree=ev_done[bk]):
                        t.wait_ge(cV.sem, ev)
                        if efree:
                            t.wait_ge(cV.sem, efree)
                        for i in range(4):
                            last = t.transpose(pT[bk][:, i * 128:(i + 1) * 128],
                                               st_sb[sp][0:128, i * 128:(i + 1) * 128], id_sb[:])
                        cP.inc(last)
                    tr_done = cP.v
                    tr_free[sp] = tr_done

                    @block.vector
                    def _(v, sp=sp, bk=bk, tr_done=tr_done, sfree=st_free[sp]):
                        v.wait_ge(cP.sem, tr_done)
                        if sfree:
                            v.wait_ge(cS.sem, sfree)
                        cV.inc(v.tensor_copy(st2_sb[sp][:], pT[bk][:, 0:512]))
                    ev_done[bk] = cV.v
                    ev2 = cV.v

                    @block.sync
                    def _(sy, b=b, n=n, oc=oc, sp=sp, ev2=ev2):
                        sy.wait_ge(cV.sem, ev2)
                        for i in range(4):
                            sy.dma_start(out=sT[b, n * 4 + i, oc, :, :],
                                         in_=st2_sb[sp][:, i * 128:(i + 1) * 128]).then_inc(cS.sem, 16)
                    cS.v += 64
                    st_free[sp] = cS.v

        @block.sync
        def _(sy):
            sy.wait_ge(cS.sem, cS.v)


def emit_l0_moe(ctx, x_h, xnT_h, sT_h, gat_h, expw_h, expb_h, ident, onesb, x1_h):
    nc = ctx.nc
    PFX = 'd_'
    with contextlib.ExitStack() as ex:
        E = ex.enter_context
        id_sb = E(nc.sbuf_tensor(PFX + "id_sb", [128, 128], BF16))
        ones_sb = E(nc.sbuf_tensor(PFX + "ones_sb", [1, 128], BF16))
        eb_sb = E(nc.sbuf_tensor(PFX + "eb_sb", [1, NE * D], BF16))
        xh_sb = E(nc.sbuf_tensor(PFX + "xh_sb", [128, NC * 1024], BF16))
        gat_sb = E(nc.sbuf_tensor(PFX + "gat_sb", [128, NT * NE], F32))
        acc = E(nc.sbuf_tensor(PFX + "acc", [128, 8 * D], BF16))
        wE = [E(nc.sbuf_tensor(PFX + f"wE{p}", [128, NC * 1024], BF16)) for p in range(2)]
        stok = E(nc.sbuf_tensor(PFX + "stok", [128, D], BF16))
        g_sb = E(nc.sbuf_tensor(PFX + "g_sb", [128, D], F32))
        xt = [E(nc.sbuf_tensor(PFX + f"xt{p}", [128, D], F32)) for p in range(2)]
        prod = E(nc.sbuf_tensor(PFX + "prod", [128, D], F32))
        pE = [E(nc.psum_tensor(PFX + f"pE{k}", [128, 512], F32)) for k in range(8)]
        cL = ctx.cL
        cW = ctx.cW
        cV = ctx.cV
        cA = ctx.cA
        cP = ctx.cP
        cS = ctx.cS
        block = ctx.block

        @block.sync
        def _(sy):
            sy.dma_start(out=id_sb[:], in_=ident[:, :]).then_inc(cL.sem, 16)
            sy.dma_start(out=ones_sb[:], in_=onesb[:, :]).then_inc(cL.sem, 16)
        cL.v += 32

        @block.gpsimd
        def _(g):
            g.dma_start(out=eb_sb[:], in_=expb_h.rearrange("e d -> (e d)").unsqueeze(0)).then_inc(cW.sem, 16)
        cW.v += 16
        prep_ld = (cL.v, cW.v)

        wE_free = {0: 0, 1: 0}
        grp_evac = {0: 0, 1: 0}
        comb_store = 0
        sil_done = 0
        last_sil = 0
        pending_wld = {}
        xh_ld = {}

        def emit_gat_load(b, pev):
            @block.sync
            def _(sy, b=b, pev=pev):
                if b > 0:
                    sy.wait_ge(cV.sem, pev)
                sy.dma_start(out=gat_sb[:], in_=gat_h[b, :, :]).then_inc(cL.sem, 16)
            cL.v += 16

        def emit_xh_load(g, pmm):
            gb, ghf = divmod(g, 2)
            @block.sync
            def _(sy, b=gb, hf=ghf, pmm=pmm, g=g):
                if g:
                    sy.wait_ge(cP.sem, pmm)
                for dc in range(NC):
                    sy.dma_start(out=xh_sb[:, dc * 1024:(dc + 1) * 1024],
                                 in_=xnT_h[b, dc * 128:(dc + 1) * 128,
                                           hf * 1024:(hf + 1) * 1024]).then_inc(cL.sem, 16)
            cL.v += 16 * NC
            xh_ld[g] = cL.v

        def emit_wE_load(g, e, ogp):
            eo = g * 16 + e * 2 + ogp
            par = eo % 2
            @block.gpsimd
            def _(gq, e=e, ogp=ogp, par=par, wf=wE_free[par]):
                if wf:
                    gq.wait_ge(cP.sem, wf)
                gq.dma_start(
                    out=wE[par][:].rearrange("p (c o) -> p c o", o=1024),
                    in_=expw_h[e, :, ogp * 1024:(ogp + 1) * 1024]
                    .rearrange("(c p) o -> p c o", p=128)).then_inc(cW.sem, 16)
            cW.v += 16
            pending_wld[(g, e, ogp)] = cW.v

        emit_gat_load(0, 0)
        emit_xh_load(0, 0)
        for g in range(2 * B2):
            b, hf = divmod(g, 2)
            ld_xh = xh_ld[g]
            sil_gate = last_sil if g else 0
            first_evac = [True]
            for e in range(NE):
                for ogp in range(2):
                    eo = g * 16 + e * 2 + ogp
                    par = eo % 2
                    if (g, e, ogp) not in pending_wld:
                        emit_wE_load(g, e, ogp)
                    w_ld = pending_wld[(g, e, ogp)]

                    for tq in range(4):
                        grp = tq % 2
                        @block.tensor
                        def _(t, par=par, e=e, ogp=ogp, tq=tq, grp=grp, w_ld=w_ld,
                              ld_xh=ld_xh, ev=grp_evac[grp]):
                            t.wait_ge(cW.sem, w_ld)
                            t.wait_ge(cL.sem, ld_xh)
                            if ev:
                                t.wait_ge(cV.sem, ev)
                            for dc in range(NC):
                                for tc2 in range(2):
                                    tl = tq * 2 + tc2
                                    for og2 in range(2):
                                        t.matmul(pE[grp * 4 + tc2 * 2 + og2][:],
                                                 xh_sb[:, dc * 1024 + tl * 128: dc * 1024 + (tl + 1) * 128],
                                                 wE[par][:, dc * 1024 + og2 * 512: dc * 1024 + (og2 + 1) * 512],
                                                 start=(dc == 0), stop=False)
                            for tc2 in range(2):
                                for og2 in range(2):
                                    og = ogp * 2 + og2
                                    last = t.matmul(pE[grp * 4 + tc2 * 2 + og2][:], ones_sb[0:1, :],
                                                    eb_sb[0:1, e * D + og * 512: e * D + (og + 1) * 512],
                                                    start=False, stop=True)
                            cP.inc(last)
                        mm_done = cP.v

                        @block.vector
                        def _(v, hf=hf, e=e, ogp=ogp, tq=tq, grp=grp, mm_done=mm_done,
                              fe=first_evac[0], sg=sil_gate):
                            v.wait_ge(cP.sem, mm_done)
                            if fe and sg:
                                v.wait_ge(cA.sem, sg)
                            for tc2 in range(2):
                                for og2 in range(2):
                                    og = ogp * 2 + og2
                                    tl = tq * 2 + tc2
                                    tg = hf * 8 + tl
                                    gcol = gat_sb[:, tg * NE + e: tg * NE + e + 1]
                                    dst = acc[:, tl * D + og * 512: tl * D + (og + 1) * 512]
                                    if e == 0:
                                        last = v.tensor_scalar(dst, pE[grp * 4 + tc2 * 2 + og2][:], gcol,
                                                               None, op0=ALU.mult)
                                    else:
                                        last = v.scalar_tensor_tensor(dst, pE[grp * 4 + tc2 * 2 + og2][:],
                                                                      gcol, dst,
                                                                      op0=ALU.mult, op1=ALU.add)
                            cV.inc(last)
                        grp_evac[grp] = cV.v
                        first_evac[0] = False
                    wE_free[par] = cP.v

            expert_evacs_done = cV.v
            if g + 1 < 2 * B2:
                nb, nhf = divmod(g + 1, 2)
                if nhf == 0:
                    emit_gat_load(nb, cV.v)
                emit_xh_load(g + 1, cP.v)
                emit_wE_load(g + 1, 0, 0)
                emit_wE_load(g + 1, 0, 1)

            for tc in range(8):
                tg = hf * 8 + tc
                par = tc % 2
                @block.sync
                def _(sy, b=b, tg=tg, par=par, cs=comb_store):
                    if cs:
                        sy.wait_ge(cS.sem, cs)
                    sy.dma_start(out=xt[par][:],
                                 in_=x_h[b, tg * 128:(tg + 1) * 128, :]).then_inc(cL.sem, 16)
                @block.gpsimd
                def _(gq, b=b, tg=tg, sd=sil_done):
                    if sd:
                        gq.wait_ge(cV.sem, sd)
                    gq.dma_start(out=stok[:].rearrange("p (c t) -> p c t", t=128),
                                 in_=sT_h[b, tg].rearrange("c p t -> p c t")).then_inc(cW.sem, 16)
                cL.v += 16
                cW.v += 16
                ld_here = (cL.v, cW.v)

                cA.v += 1
                a_sil = cA.v
                @block.scalar
                def _(sc, tc=tc, ev=expert_evacs_done, sd=sil_done):
                    sc.wait_ge(cV.sem, ev)
                    if sd:
                        sc.wait_ge(cV.sem, sd)
                    sc.activation(g_sb[:], acc[:, tc * D:(tc + 1) * D], AF.Silu).then_inc(cA.sem, 1)

                @block.vector
                def _(v, par=par, a_sil=a_sil, ld=ld_here):
                    v.wait_ge(cA.sem, a_sil)
                    v.wait_ge(cL.sem, ld[0])
                    v.wait_ge(cW.sem, ld[1])
                    cV.inc(v.scalar_tensor_tensor(prod[:], stok[:], 1.0, g_sb[:],
                                                  op0=ALU.mult, op1=ALU.mult))
                    v.wait_ge(cV.sem, cV.v)
                    cV.inc(v.tensor_tensor(xt[par][:], xt[par][:], prod[:], op=ALU.add))
                sil_done = cV.v

                @block.sync
                def _(sy, b=b, tg=tg, par=par, sd=sil_done):
                    sy.wait_ge(cV.sem, sd)
                    sy.dma_start(out=x1_h[b, tg * 128:(tg + 1) * 128, :],
                                 in_=xt[par][:]).then_inc(cS.sem, 16)
                cS.v += 16
                if tc >= 1:
                    comb_store = cS.v - 16
                else:
                    comb_store = 0
            last_sil = cA.v

        @block.sync
        def _(sy):
            sy.wait_ge(cS.sem, cS.v)


def emit_l1_ln(ctx, x_h, ident, xtok_h, xnT_h):
    nc = ctx.nc
    PFX = 'e_'
    with contextlib.ExitStack() as ex:
        E = ex.enter_context
        NP = 4
        xt = [E(nc.sbuf_tensor(PFX + f"xt{p}", [128, D], F32)) for p in range(NP)]
        scr = E(nc.sbuf_tensor(PFX + "scr", [128, D], F32))
        xnb = [E(nc.sbuf_tensor(PFX + f"xnb{p}", [128, D], BF16)) for p in range(NP)]
        xnT_sb = E(nc.sbuf_tensor(PFX + "xnT_sb", [128, NC * S], BF16))
        id_sb = E(nc.sbuf_tensor(PFX + "id_sb", [128, 128], BF16))
        mv = E(nc.sbuf_tensor(PFX + "mv", [128, 4 * NP], F32))
        lnv = E(nc.sbuf_tensor(PFX + "lnv", [128, NP], F32))
        rstd = E(nc.sbuf_tensor(PFX + "rstd", [128, NP], F32))
        pb = [E(nc.psum_tensor(PFX + f"pb{k}", [128, 1024], BF16)) for k in range(2)]
        cL = ctx.cL
        cV = ctx.cV
        cA = ctx.cA
        cP = ctx.cP
        cS = ctx.cS
        block = ctx.block

        @block.sync
        def _(sy):
            sy.dma_start(out=id_sb[:], in_=ident[:, :]).then_inc(cL.sem, 16)
        cL.v += 16

        v_xn_hist = {}
        ev_bank = {0: 0, 1: 0}
        tr_par = {p: 0 for p in range(NP)}
        prev_store = {p: 0 for p in range(NP)}
        for b in range(B2):
            for i in range(NT):
                it = b * NT + i
                par = it % NP
                o = par * 4
                @block.sync
                def _(sy, b=b, i=i, par=par, it=it):
                    if it >= NP:
                        sy.wait_ge(cV.sem, v_xn_hist[it - NP])
                    sy.dma_start(out=xt[par][:],
                                 in_=x_h[b, i * 128:(i + 1) * 128, :]).then_inc(cL.sem, 16)
                cL.v += 16
                ld_here = cL.v

                cA.v += 1
                a_st = cA.v
                @block.scalar
                def _(sc, par=par, o=o, ld=ld_here, pvx=v_xn_hist.get(it - NP, 0)):
                    sc.wait_ge(cL.sem, ld)
                    if pvx:
                        sc.wait_ge(cV.sem, pvx)
                    sc.activation(scr[:], xt[par][:], AF.Identity, accum_out=mv[:, o:o + 1])
                    sc.activation(scr[:], xt[par][:], AF.Square,
                                  accum_out=mv[:, o + 3:o + 4]).then_inc(cA.sem, 1)

                @block.vector
                def _(v, a_st=a_st, o=o):
                    v.wait_ge(cA.sem, a_st)
                    v.tensor_scalar(mv[:, o:o + 1], mv[:, o:o + 1], 1.0 / D, None, op0=ALU.mult)
                    cV.inc(v.tensor_scalar(mv[:, o + 3:o + 4], mv[:, o + 3:o + 4], 1.0 / D, EPS,
                                           op0=ALU.mult, op1=ALU.add))
                    v.wait_ge(cV.sem, cV.v)
                    cV.inc(v.tensor_tensor(mv[:, o + 2:o + 3], mv[:, o:o + 1], mv[:, o:o + 1],
                                           op=ALU.mult))
                    v.wait_ge(cV.sem, cV.v)
                    cV.inc(v.tensor_tensor(mv[:, o + 1:o + 2], mv[:, o + 3:o + 4],
                                           mv[:, o + 2:o + 3], op=ALU.subtract))
                v_bn = cV.v

                cA.v += 2
                a_rstd = cA.v
                @block.scalar
                def _(sc, v_bn=v_bn, a1=a_rstd - 1, par=par, o=o):
                    sc.wait_ge(cV.sem, v_bn)
                    sc.activation(lnv[:, par:par + 1], mv[:, o + 1:o + 2], AF.Ln).then_inc(cA.sem, 1)
                    sc.wait_ge(cA.sem, a1)
                    sc.activation(rstd[:, par:par + 1], lnv[:, par:par + 1], AF.Exp,
                                  scale=-0.5).then_inc(cA.sem, 1)

                @block.vector
                def _(v, par=par, o=o, a_rstd=a_rstd, st=prev_store[par], tp=tr_par[par]):
                    v.wait_ge(cA.sem, a_rstd)
                    if st:
                        v.wait_ge(cS.sem, st)
                    if tp:
                        v.wait_ge(cP.sem, tp)
                    cV.inc(v.tensor_scalar(xnb[par][:], xt[par][:], mv[:, o:o + 1],
                                           rstd[:, par:par + 1],
                                           op0=ALU.subtract, op1=ALU.mult))
                v_xn_hist[it] = cV.v

                @block.sync
                def _(sy, b=b, i=i, par=par, vx=cV.v):
                    sy.wait_ge(cV.sem, vx)
                    sy.dma_start(out=xtok_h[b, i * 128:(i + 1) * 128, :],
                                 in_=xnb[par][:]).then_inc(cS.sem, 16)
                cS.v += 16
                prev_store[par] = cS.v

                for half in range(2):
                    bk = half
                    @block.tensor
                    def _(t, par=par, half=half, bk=bk, vx=v_xn_hist[it], ef=ev_bank[bk], it=it):
                        t.wait_ge(cV.sem, vx)
                        if ef:
                            t.wait_ge(cV.sem, ef)
                        if it == 0 and half == 0:
                            t.wait_ge(cL.sem, 16)
                        for s_idx in range(8):
                            dc = half * 8 + s_idx
                            last = t.transpose(pb[bk][:, s_idx * 128:(s_idx + 1) * 128],
                                               xnb[par][0:128, dc * 128:(dc + 1) * 128], id_sb[:])
                        cP.inc(last)
                    tr = cP.v

                    @block.vector
                    def _(v, i=i, half=half, bk=bk, tr=tr):
                        v.wait_ge(cP.sem, tr)
                        cV.inc(v.tensor_copy(
                            xnT_sb[:].rearrange("p (c t) -> p c t", t=S)[:, half * 8:half * 8 + 8,
                                                                         i * 128:(i + 1) * 128],
                            pb[bk][:].rearrange("p (c t) -> p c t", t=128)))
                    ev_bank[bk] = cV.v
                tr_par[par] = cP.v

            last_ev = cV.v
            @block.sync
            def _(sy, b=b, last_ev=last_ev):
                sy.wait_ge(cV.sem, last_ev)
                for dc in range(NC):
                    sy.dma_start(out=xnT_h[b, dc * 128:(dc + 1) * 128, :],
                                 in_=xnT_sb[:, dc * S:(dc + 1) * S]).then_inc(cS.sem, 16)
            cS.v += 16 * NC
            if b + 1 < B2:
                st_b = cS.v
                @block.tensor
                def _(t, st_b=st_b):
                    t.wait_ge(cS.sem, st_b)

        @block.sync
        def _(sy):
            sy.wait_ge(cS.sem, cS.v)


def make_wprep_stepper(ctx, E, id_sb, id_ld, wg_h, wu_h, wd_h, wgT, wuT, wdT):
    nc = ctx.nc
    PFX = 'f_'
    w_sb = [E(nc.sbuf_tensor(PFX + f"w_sb{p}", [128, FI], BF16)) for p in range(2)]
    stg = [E(nc.sbuf_tensor(PFX + f"stg{p}", [128, FI], BF16)) for p in range(2)]
    pb = [E(nc.psum_tensor(PFX + f"pb{k}", [128, 1024], BF16)) for k in range(2)]
    cW, cV, cP, cS = ctx.cW, ctx.cV, ctx.cP, ctx.cS
    block = ctx.block

    gu, dd = [], []
    for li in range(NI):
        gu.append(("g", wg_h, li, D // 128, wgT))
        gu.append(("u", wu_h, li, D // 128, wuT))
    for lh in range(NC):
        dd.append(("d", wd_h, lh, FI // 128, wdT))
    jobs = []
    for i in range(NC):
        jobs.extend(gu[i * 8:(i + 1) * 8])
        jobs.append(dd[i])

    state = {"jn": 0, "mm_free": {0: 0, 1: 0}, "ev_free": {0: 0, 1: 0},
             "st_free": {0: 0, 1: 0}}

    def emit_next(k):
        mm_free, ev_free, st_free = state["mm_free"], state["ev_free"], state["st_free"]
        for _ in range(k):
            jn = state["jn"]
            if jn >= len(jobs):
                return
            state["jn"] = jn + 1
            kind, src_h, loc, ncol, dst_h = jobs[jn]
            par = jn % 2
            W = ncol * 128

            @block.gpsimd
            def _(g, src_h=src_h, loc=loc, W=W, par=par, mf=mm_free[par]):
                if mf:
                    g.wait_ge(cP.sem, mf)
                g.dma_start(out=w_sb[par][:, 0:W],
                            in_=src_h[loc * 128:(loc + 1) * 128, 0:W]).then_inc(cW.sem, 16)
            cW.v += 16
            wl = cW.v

            for grp in range(ncol // 8):
                bk = grp % 2
                @block.tensor
                def _(t, par=par, grp=grp, bk=bk, wl=wl, ef=ev_free[bk],
                      first=(jn == 0 and grp == 0)):
                    t.wait_ge(cW.sem, wl)
                    if first:
                        t.wait_ge(cL_id[0].sem, cL_id[1])
                    if ef:
                        t.wait_ge(cV.sem, ef)
                    for s_idx in range(8):
                        cc = grp * 8 + s_idx
                        last = t.transpose(pb[bk][:, s_idx * 128:(s_idx + 1) * 128],
                                           w_sb[par][0:128, cc * 128:(cc + 1) * 128], id_sb[:])
                    cP.inc(last)
                tr = cP.v
                if grp == ncol // 8 - 1:
                    mm_free[par] = tr

                @block.vector
                def _(v, par=par, grp=grp, bk=bk, tr=tr, sf=st_free[par]):
                    v.wait_ge(cP.sem, tr)
                    if sf and grp < 2:
                        v.wait_ge(cS.sem, sf)
                    cV.inc(v.tensor_scalar(stg[par][:, grp * 1024:(grp + 1) * 1024],
                                           pb[bk][:], 128.0, None, op0=ALU.subtract))
                ev_free[bk] = cV.v
            ev_all = cV.v

            @block.sync
            def _(sy, dst_h=dst_h, loc=loc, ncol=ncol, par=par, ev_all=ev_all):
                sy.wait_ge(cV.sem, ev_all)
                sy.dma_start(out=dst_h[loc].rearrange("p c i -> p (c i)"),
                             in_=stg[par][:, 0:ncol * 128]).then_inc(cS.sem, 16)
            cS.v += 16
            st_free[par] = cS.v

    cL_id = (ctx.cL, id_ld)
    return emit_next


def emit_l1_attn(ctx, xtok_h, ident, cmask_h, s1_h):
    nc = ctx.nc
    PFX = 'g_'
    RSC = float(HD ** -0.25)
    with contextlib.ExitStack() as ex:
        E = ex.enter_context
        xt_sb = E(nc.sbuf_tensor(PFX + "xt_sb", [128, NT * D], BF16))
        stok = E(nc.sbuf_tensor(PFX + "stok", [128, NT * D], BF16))
        kT = [E(nc.sbuf_tensor(PFX + f"kT{s}", [128, S], BF16)) for s in range(2)]
        p_sb = [[E(nc.sbuf_tensor(PFX + f"p_sb{s}{p}", [128, S], BF16)) for p in range(2)]
                for s in range(2)]
        pT_sb = [E(nc.sbuf_tensor(PFX + f"pT_sb{s}", [128, NT * 128], BF16)) for s in range(2)]
        id_sb = E(nc.sbuf_tensor(PFX + "id_sb", [128, 128], BF16))
        cm_sb = E(nc.sbuf_tensor(PFX + "cm_sb", [128, 128], F32))
        nmax = E(nc.sbuf_tensor(PFX + "nmax", [128, 4], F32))
        rsum = E(nc.sbuf_tensor(PFX + "rsum", [128, 4], F32))
        rr = E(nc.sbuf_tensor(PFX + "rr", [128, 4], F32))
        pS = E(nc.psum_tensor(PFX + "pS", [128, 2048], F32))
        pT0 = E(nc.psum_tensor(PFX + "pT0", [128, 1024], BF16))
        pT1 = E(nc.psum_tensor(PFX + "pT1", [128, 1024], BF16))
        pO = [E(nc.psum_tensor(PFX + f"pO{s}", [128, 512], F32)) for s in range(2)]
        cL, cW, cV, cA, cP, cS = ctx.cL, ctx.cW, ctx.cV, ctx.cA, ctx.cP, ctx.cS
        block = ctx.block
        pT = [pT0, pT1]

        @block.sync
        def _(sy):
            sy.dma_start(out=id_sb[:], in_=ident[:, :]).then_inc(cL.sem, 16)
            sy.dma_start(out=cm_sb[:], in_=cmask_h[:, :]).then_inc(cL.sem, 16)
        cL.v += 32

        bank_free = {0: None, 1: None}
        bkc = [0]
        kT_last_use = [0, 0]
        last_pS_exp = [0]
        for b in range(B2):
            @block.sync
            def _(sy, b=b, pdone=cS.v):
                if b > 0:
                    sy.wait_ge(cS.sem, pdone)
                for tc in range(NT):
                    sy.dma_start(out=xt_sb[:, tc * D:(tc + 1) * D],
                                 in_=xtok_h[b, tc * 128:(tc + 1) * 128, :]).then_inc(cL.sem, 16)
            cL.v += 16 * NT
            ld_x = cL.v

            for hp in range(H // 2):
                heads = [hp * 2, hp * 2 + 1]
                kT_ready = [0, 0]
                for s in range(2):
                    h = heads[s]
                    for half in range(2):
                        bk = bkc[0]; bkc[0] ^= 1
                        @block.tensor
                        def _(t, h=h, s=s, half=half, bk=bk, ld_x=ld_x,
                              bf=bank_free[bk], ku=kT_last_use[s]):
                            t.wait_ge(cL.sem, ld_x)
                            if half == 0 and ku:
                                t.wait_ge(cP.sem, ku)
                            if bf is not None:
                                t.wait_ge(bf[0].sem, bf[1])
                            for s_idx in range(8):
                                tc = half * 8 + s_idx
                                last = t.transpose(pT[bk][:, s_idx * 128:(s_idx + 1) * 128],
                                                   xt_sb[0:128, tc * D + h * 128: tc * D + (h + 1) * 128],
                                                   id_sb[:])
                            cP.inc(last)
                        tr = cP.v

                        @block.vector
                        def _(v, s=s, half=half, bk=bk, tr=tr):
                            v.wait_ge(cP.sem, tr)
                            cV.inc(v.tensor_scalar(
                                kT[s][:, half * 1024:(half + 1) * 1024], pT[bk][:], RSC,
                                None, op0=ALU.mult))
                        bank_free[bk] = (cV, cV.v)
                    kT_ready[s] = cV.v

                exp_hist = [{}, {}]
                ptg_hist = [{}, {}]
                rr_hist = [{}, {}]
                o_evac_done = [0, 0]

                def emit_softmax(s, qi):
                    qp = qi % 2
                    col = s * 2 + qp
                    kw = (qi + 1) * 128
                    nblk = (kw + 511) // 512
                    @block.tensor
                    def _(t, s=s, qi=qi, kw=kw, nblk=nblk, ed=last_pS_exp[0],
                          kr=kT_ready[s]):
                        t.wait_ge(cV.sem, kr)
                        if ed:
                            t.wait_ge(cA.sem, ed)
                        for n in range(nblk):
                            w = min(512, kw - n * 512)
                            last = t.matmul(pS[:, n * 512:n * 512 + w],
                                            kT[s][:, qi * 128:(qi + 1) * 128],
                                            kT[s][:, n * 512:n * 512 + w], start=True, stop=True)
                        cP.inc(last)
                    s_done = cP.v
                    kT_last_use[s] = cP.v

                    @block.vector
                    def _(v, qi=qi, col=col, kw=kw, s_done=s_done):
                        v.wait_ge(cP.sem, s_done)
                        cV.inc(v.tensor_tensor(pS[:, qi * 128:(qi + 1) * 128],
                                               pS[:, qi * 128:(qi + 1) * 128], cm_sb[:], op=ALU.add))
                        v.wait_ge(cV.sem, cV.v)
                        cV.inc(v.tensor_reduce(nmax[:, col:col + 1], pS[:, 0:kw], axis=AX.X,
                                               op=ALU.max, negate=True))
                    max_done = cV.v

                    cA.v += 1
                    exp_hist[s][qi] = cA.v
                    last_pS_exp[0] = cA.v
                    @block.scalar
                    def _(sc, s=s, qi=qi, qp=qp, col=col, kw=kw, max_done=max_done,
                          ptg=ptg_hist[s].get(qi - 2, 0)):
                        sc.wait_ge(cV.sem, max_done)
                        if ptg:
                            sc.wait_ge(cP.sem, ptg)
                        sc.activation(p_sb[s][qp][:, 0:kw], pS[:, 0:kw], AF.Exp,
                                      bias=nmax[:, col:col + 1],
                                      accum_out=rsum[:, col:col + 1]).then_inc(cA.sem, 1)

                    @block.vector
                    def _(v, col=col, ed=exp_hist[s][qi]):
                        v.wait_ge(cA.sem, ed)
                        cV.inc(v.reciprocal(rr[:, col:col + 1], rsum[:, col:col + 1]))
                        v.wait_ge(cV.sem, cV.v)
                    rr_hist[s][qi] = cV.v

                def emit_pv(s, qi):
                    qp = qi % 2
                    col = s * 2 + qp
                    h = heads[s]
                    ntile = qi + 1
                    first_grp = True
                    for base in range(0, ntile, 8):
                        cnt = min(8, ntile - base)
                        bk = bkc[0]; bkc[0] ^= 1
                        @block.tensor
                        def _(t, s=s, qp=qp, base=base, cnt=cnt, bk=bk, bf=bank_free[bk],
                              fg=first_grp, ed=exp_hist[s][qi]):
                            if fg:
                                t.wait_ge(cA.sem, ed)
                            if bf is not None:
                                t.wait_ge(bf[0].sem, bf[1])
                            for s_idx in range(cnt):
                                kc = base + s_idx
                                last = t.transpose(pT[bk][:, s_idx * 128:(s_idx + 1) * 128],
                                                   p_sb[s][qp][0:128, kc * 128:(kc + 1) * 128],
                                                   id_sb[:])
                            cP.inc(last)
                        first_grp = False
                        tr = cP.v

                        @block.vector
                        def _(v, s=s, base=base, cnt=cnt, bk=bk, tr=tr):
                            v.wait_ge(cP.sem, tr)
                            cV.inc(v.tensor_copy(
                                pT_sb[s][:].rearrange("p (c t) -> p c t", t=128)[:, base:base + cnt, :],
                                pT[bk][:, 0:cnt * 128].rearrange("p (c t) -> p c t", t=128)))
                        bank_free[bk] = (cV, cV.v)
                    pt_ready = cV.v
                    ptg_hist[s][qi] = cP.v

                    @block.tensor
                    def _(t, s=s, qi=qi, h=h, pt_ready=pt_ready, oed=o_evac_done[s]):
                        t.wait_ge(cV.sem, pt_ready)
                        if oed:
                            t.wait_ge(cA.sem, oed)
                        for kc in range(qi + 1):
                            last = t.matmul(pO[s][:, 0:128], pT_sb[s][:, kc * 128:(kc + 1) * 128],
                                            xt_sb[:, kc * D + h * 128: kc * D + (h + 1) * 128],
                                            start=(kc == 0), stop=(kc == qi))
                        cP.inc(last)
                    pv_done = cP.v

                    cA.v += 1
                    @block.scalar
                    def _(sc, s=s, qi=qi, col=col, h=h, pv_done=pv_done, rrd=rr_hist[s][qi]):
                        sc.wait_ge(cP.sem, pv_done)
                        sc.wait_ge(cV.sem, rrd)
                        sc.activation(stok[:, qi * D + h * 128: qi * D + (h + 1) * 128],
                                      pO[s][:, 0:128], AF.Identity,
                                      scale=rr[:, col:col + 1]).then_inc(cA.sem, 1)
                    o_evac_done[s] = cA.v

                for qi in range(NT + 1):
                    for s in range(2):
                        if qi >= 1:
                            emit_pv(s, qi - 1)
                        if qi < NT:
                            emit_softmax(s, qi)

            oe = cA.v
            @block.sync
            def _(sy, b=b, oe=oe):
                sy.wait_ge(cA.sem, oe)
                for tc in range(NT):
                    sy.dma_start(out=s1_h[b, tc * 128:(tc + 1) * 128, :],
                                 in_=stok[:, tc * D:(tc + 1) * D]).then_inc(cS.sem, 16)
            cS.v += 16 * NT

        @block.sync
        def _(sy):
            sy.wait_ge(cS.sem, cS.v)

def emit_l1_ffn(ctx, x_h, xnT_h, s1_h, wgT_h, wuT_h, wdT_h, gsc_h, usc_h, dsc_h, ident, out_h):
    nc = ctx.nc
    PFX = 'h_'
    with contextlib.ExitStack() as ex:
        E = ex.enter_context
        id_sb = E(nc.sbuf_tensor(PFX + "id_sb", [128, 128], BF16))
        xb_sb = [E(nc.sbuf_tensor(PFX + f"xb_sb{p}", [128, NC * 512], BF16)) for p in range(2)]
        ht = E(nc.sbuf_tensor(PFX + "ht", [128, NI * 512], BF16))
        wg_sb = [E(nc.sbuf_tensor(PFX + f"wg_sb{p}", [128, NC * 128], BF16)) for p in range(2)]
        wu_sb = [E(nc.sbuf_tensor(PFX + f"wu_sb{p}", [128, NC * 128], BF16)) for p in range(2)]
        wd_sb = [E(nc.sbuf_tensor(PFX + f"wd_sb{p}", [128, 32 * 128], BF16)) for p in range(4)]
        sg_sb = [E(nc.sbuf_tensor(PFX + f"sg_sb{p}", [128, 512], BF16)) for p in range(2)]
        gts_sb = [E(nc.sbuf_tensor(PFX + f"gts_sb{p}", [128, 512], BF16)) for p in range(2)]
        gtok = E(nc.sbuf_tensor(PFX + "gtok", [128, 4 * D], BF16))
        s1r = [E(nc.sbuf_tensor(PFX + f"s1r{p}", [128, D], BF16)) for p in range(2)]
        xt = [E(nc.sbuf_tensor(PFX + f"xtc{p}", [128, D], F32)) for p in range(2)]
        prod = E(nc.sbuf_tensor(PFX + "prod", [128, D], F32))
        gsc = E(nc.sbuf_tensor(PFX + "gsc_sb", [128, NI], F32))
        usc = E(nc.sbuf_tensor(PFX + "usc_sb", [128, NI], F32))
        dsc = E(nc.sbuf_tensor(PFX + "dsc_sb", [128, NC], F32))
        pG = [E(nc.psum_tensor(PFX + f"pG{p}", [128, 512], F32)) for p in range(2)]
        pU = [E(nc.psum_tensor(PFX + f"pU{p}", [128, 512], F32)) for p in range(2)]
        pD = [E(nc.psum_tensor(PFX + f"pD{p}", [128, 512], F32)) for p in range(2)]
        pTT = [E(nc.psum_tensor(PFX + f"pTT{p}", [128, 1024], BF16)) for p in range(2)]
        cL = ctx.cL
        cW = ctx.cW
        cV = ctx.cV
        cA = ctx.cA
        cP = ctx.cP
        cS = ctx.cS
        block = ctx.block

        @block.sync
        def _(sy):
            sy.dma_start(out=id_sb[:], in_=ident[:, :]).then_inc(cL.sem, 16)
            sy.dma_start(out=gsc[:], in_=gsc_h[:, :]).then_inc(cL.sem, 16)
            sy.dma_start(out=usc[:], in_=usc_h[:, :]).then_inc(cL.sem, 16)
            sy.dma_start(out=dsc[:], in_=dsc_h[:, :]).then_inc(cL.sem, 16)
        cL.v += 64

        wld = {}
        wdld = {}
        gu_free = {0: (None, 0), 1: (None, 0)}
        st_free = {0: 0, 1: 0}
        lastgu = {}
        xb_ld = {}
        last_down_prev_tb = 0

        def emit_xb_load(tbi):
            b_, tb_ = divmod(tbi, 4)
            par = tbi % 2
            gate = lastgu.get(tbi - 2, 0)
            @block.sync
            def _(sy, b=b_, tb=tb_, par=par, gate=gate):
                if gate:
                    sy.wait_ge(cP.sem, gate)
                for hc in range(NC):
                    sy.dma_start(out=xb_sb[par][:, hc * 512:(hc + 1) * 512],
                                 in_=xnT_h[b, hc * 128:(hc + 1) * 128,
                                           tb * 512:(tb + 1) * 512]).then_inc(cL.sem, 16)
            cL.v += 16 * NC
            xb_ld[tbi] = cL.v

        emit_xb_load(0)
        for b in range(B2):
            for tb in range(4):
                tbi = b * 4 + tb
                xpar = tbi % 2

                silu_done = {}
                ht_done = 0
                for ic in range(NI):
                    par = ic % 2
                    @block.gpsimd
                    def _(g, ic=ic, par=par, gf=gu_free[par]):
                        if gf[0] is not None:
                            g.wait_ge(cP.sem, gf[1])
                        g.dma_start(out=wg_sb[par][:], in_=wgT_h[ic].rearrange("p c i -> p (c i)")
                                    ).then_inc(cW.sem, 16)
                        g.dma_start(out=wu_sb[par][:], in_=wuT_h[ic].rearrange("p c i -> p (c i)")
                                    ).then_inc(cW.sem, 16)
                    cW.v += 32
                    wld[ic] = cW.v

                    @block.tensor
                    def _(t, ic=ic, par=par, w_ld=wld[ic], xld=xb_ld[tbi], xpar=xpar,
                          sfree=silu_done.get(ic - 2, 0), ufree=0):
                        t.wait_ge(cW.sem, w_ld)
                        t.wait_ge(cL.sem, xld)
                        if sfree:
                            t.wait_ge(cV.sem, sfree)
                        for hc in range(NC):
                            lastg = t.matmul(pG[par][:], wg_sb[par][:, hc * 128:(hc + 1) * 128],
                                             xb_sb[xpar][:, hc * 512:(hc + 1) * 512],
                                             start=(hc == 0), stop=(hc == NC - 1))
                        cP.inc(lastg)
                        for hc in range(NC):
                            lastu = t.matmul(pU[par][:], wu_sb[par][:, hc * 128:(hc + 1) * 128],
                                             xb_sb[xpar][:, hc * 512:(hc + 1) * 512],
                                             start=(hc == 0), stop=(hc == NC - 1))
                        cP.inc(lastu)
                    gu_free[par] = (cP, cP.v)
                    g_mm = cP.v - 1
                    u_mm = cP.v

                    cA.v += 1
                    a_sil = cA.v
                    @block.scalar
                    def _(sc, ic=ic, par=par, g_mm=g_mm, sfree=silu_done.get(ic - 2, 0)):
                        sc.wait_ge(cP.sem, g_mm)
                        if sfree:
                            sc.wait_ge(cV.sem, sfree)
                        sc.activation(sg_sb[par][:], pG[par][:], AF.Silu,
                                      scale=gsc[:, ic:ic + 1]).then_inc(cA.sem, 1)

                    @block.vector
                    def _(v, ic=ic, par=par, u_mm=u_mm, a_sil=a_sil,
                          hfree=(last_down_prev_tb if ic == 0 else 0)):
                        v.wait_ge(cP.sem, u_mm)
                        v.wait_ge(cA.sem, a_sil)
                        if hfree:
                            v.wait_ge(cP.sem, hfree)
                        cV.inc(v.scalar_tensor_tensor(ht[:, ic * 512:(ic + 1) * 512],
                                                      pU[par][:], usc[:, ic:ic + 1], sg_sb[par][:],
                                                      op0=ALU.mult, op1=ALU.mult))
                    silu_done[ic] = cV.v
                lastgu[tbi] = cP.v
                if tbi + 1 < B2 * 4:
                    emit_xb_load(tbi + 1)
                ht_ready = cV.v

                gt_tr = {}
                ev_done = {0: 0, 1: 0}
                dn_free = {0: (None, 0), 1: (None, 0), 2: (None, 0), 3: (None, 0)}
                for hc in range(NC):
                    for half in range(2):
                        par = (hc * 2 + half) % 4
                        @block.sync
                        def _(sy, hc=hc, half=half, par=par, df=dn_free[par]):
                            if df[0] is not None:
                                sy.wait_ge(cP.sem, df[1])
                            sy.dma_start(out=wd_sb[par][:],
                                         in_=wdT_h[hc, :, half * 32:(half + 1) * 32, :]
                                         .rearrange("p c i -> p (c i)")).then_inc(cL.sem, 16)
                        cL.v += 16
                        wdld[(hc, half)] = cL.v

                    pdp = hc % 2
                    @block.tensor
                    def _(t, hc=hc, pdp=pdp, ht_ready=ht_ready,
                          sfree=gt_tr.get(hc - 2, 0)):
                        t.wait_ge(cV.sem, ht_ready)
                        if sfree:
                            t.wait_ge(cP.sem, sfree)
                        for half in range(2):
                            par = (hc * 2 + half) % 4
                            t.wait_ge(cL.sem, wdld[(hc, half)])
                            for k in range(32):
                                ic = half * 32 + k
                                last = t.matmul(pD[pdp][:], wd_sb[par][:, k * 128:(k + 1) * 128],
                                                ht[:, ic * 512:(ic + 1) * 512],
                                                start=(ic == 0), stop=(ic == NI - 1))
                        cP.inc(last)
                    dn_free[(hc * 2) % 4] = (cP, cP.v)
                    dn_free[(hc * 2 + 1) % 4] = (cP, cP.v)
                    d_mm = cP.v

                    gpar = hc % 2
                    cA.v += 1
                    a_gt = cA.v
                    @block.scalar
                    def _(sc, hc=hc, pdp=pdp, gpar=gpar, d_mm=d_mm, tfree=gt_tr.get(hc - 2, 0)):
                        sc.wait_ge(cP.sem, d_mm)
                        if tfree:
                            sc.wait_ge(cP.sem, tfree)
                        sc.activation(gts_sb[gpar][:], pD[pdp][:], AF.Silu,
                                      scale=dsc[:, hc:hc + 1]).then_inc(cA.sem, 1)

                    bk = hc % 2
                    @block.tensor
                    def _(t, hc=hc, gpar=gpar, bk=bk, a_gt=a_gt, ef=ev_done[bk]):
                        t.wait_ge(cA.sem, a_gt)
                        if ef:
                            t.wait_ge(cV.sem, ef)
                        for i in range(4):
                            last = t.transpose(pTT[bk][:, i * 128:(i + 1) * 128],
                                               gts_sb[gpar][0:128, i * 128:(i + 1) * 128], id_sb[:])
                        cP.inc(last)
                    gt_tr[hc] = cP.v
                    tr = cP.v

                    @block.vector
                    def _(v, hc=hc, bk=bk, tr=tr, cfree=(0 if tbi == 0 else comb_done)):
                        v.wait_ge(cP.sem, tr)
                        if hc == 0 and cfree:
                            v.wait_ge(cV.sem, cfree)
                        cV.inc(v.tensor_copy(
                            gtok[:].rearrange("p (s d) -> p s d", d=D)[:, :, hc * 128:(hc + 1) * 128],
                            pTT[bk][:, 0:512].rearrange("p (s t) -> p s t", t=128)))
                    ev_done[bk] = cV.v
                last_down_prev_tb = cP.v
                g_ready = cV.v

                for tsub in range(4):
                    tg = tb * 4 + tsub
                    par = tsub % 2
                    @block.sync
                    def _(sy, b=b, tg=tg, par=par, sf=st_free[par]):
                        if sf:
                            sy.wait_ge(cS.sem, sf)
                        sy.dma_start(out=xt[par][:],
                                     in_=x_h[b, tg * 128:(tg + 1) * 128, :]).then_inc(cL.sem, 16)
                        sy.dma_start(out=s1r[par][:],
                                     in_=s1_h[b, tg * 128:(tg + 1) * 128, :]).then_inc(cL.sem, 16)
                    cL.v += 32
                    c_ld = cL.v

                    @block.vector
                    def _(v, tsub=tsub, par=par, c_ld=c_ld, g_ready=g_ready):
                        v.wait_ge(cL.sem, c_ld)
                        v.wait_ge(cV.sem, g_ready)
                        cV.inc(v.scalar_tensor_tensor(prod[:], s1r[par][:], 1.0,
                                                      gtok[:, tsub * D:(tsub + 1) * D],
                                                      op0=ALU.mult, op1=ALU.mult))
                        v.wait_ge(cV.sem, cV.v)
                        cV.inc(v.tensor_tensor(xt[par][:], xt[par][:], prod[:], op=ALU.add))
                    c_done = cV.v

                    @block.sync
                    def _(sy, b=b, tg=tg, par=par, c_done=c_done):
                        sy.wait_ge(cV.sem, c_done)
                        sy.dma_start(out=out_h[b, tg * 128:(tg + 1) * 128, :],
                                     in_=xt[par][:]).then_inc(cS.sem, 16)
                    cS.v += 16
                    st_free[par] = cS.v
                comb_done = cV.v

        @block.sync
        def _(sy):
            sy.wait_ge(cS.sem, cS.v)


class DeferBlock:
    def __init__(self, real):
        self.real = real
        self.squeue = []

    def scalar(self, f):
        self.squeue.append(f)

    def flush(self):
        if self.squeue:
            fns = self.squeue
            self.squeue = []
            def runner(sc, fns=fns):
                for f in fns:
                    f(sc)
            self.real.scalar(runner)

    def tensor(self, f):
        self.real.tensor(f)

    def vector(self, f):
        self.real.vector(f)

    def sync(self, f):
        self.real.sync(f)

    def gpsimd(self, f):
        self.real.gpsimd(f)


def barrier(ctx):
    ctx.block.flush()
    cs = [ctx.cL, ctx.cW, ctx.cV, ctx.cA, ctx.cP, ctx.cS]
    vals = [(c.sem, c.v) for c in cs]
    for engname in ("tensor", "vector", "scalar", "sync", "gpsimd"):
        def f(e, vals=vals):
            for s, v in vals:
                if v:
                    e.wait_ge(s, v)
        getattr(ctx.block.real, engname)(f)


class _Ctx:
    pass


def build_fused():
    nc = bass.Bass("TRN2", target_bir_lowering=False)
    x = nc.dram_tensor("x", [B2, S, D], F32, kind="ExternalInput").ap()
    wq_h = nc.dram_tensor("wq", [D, D], F32, kind="ExternalInput").ap()
    wk_h = nc.dram_tensor("wk", [D, D], F32, kind="ExternalInput").ap()
    wv_h = nc.dram_tensor("wv", [D, D], F32, kind="ExternalInput").ap()
    wo_h = nc.dram_tensor("wo", [D, D], F32, kind="ExternalInput").ap()
    gw = nc.dram_tensor("gw", [NE, D], F32, kind="ExternalInput").ap()
    expw_h = nc.dram_tensor("expw", [NE, D, D], F32, kind="ExternalInput").ap()
    expb_h = nc.dram_tensor("expb", [NE, D], F32, kind="ExternalInput").ap()
    wg_h = nc.dram_tensor("l1wg", [FI, D], I32, kind="ExternalInput").ap()
    wu_h = nc.dram_tensor("l1wu", [FI, D], I32, kind="ExternalInput").ap()
    wd_h = nc.dram_tensor("l1wd", [D, FI], I32, kind="ExternalInput").ap()
    gsc_h = nc.dram_tensor("gsc", [128, NI], F32, kind="ExternalInput").ap()
    usc_h = nc.dram_tensor("usc", [128, NI], F32, kind="ExternalInput").ap()
    dsc_h = nc.dram_tensor("dsc", [128, NC], F32, kind="ExternalInput").ap()
    ident = nc.dram_tensor("ident", [128, 128], BF16, kind="ExternalInput").ap()
    identf = nc.dram_tensor("identf", [128, 128], F32, kind="ExternalInput").ap()
    cmask_h = nc.dram_tensor("cmask", [128, 128], F32, kind="ExternalInput").ap()
    cmaskt_h = nc.dram_tensor("cmaskt", [128, 128], F32, kind="ExternalInput").ap()
    onesb = nc.dram_tensor("onesb", [1, 128], BF16, kind="ExternalInput").ap()
    out_h = nc.dram_tensor("out", [B2, S, D], F32, kind="ExternalOutput").ap()
    xnT = nc.dram_tensor("i_xnT", [B2, D, S], BF16, kind="Internal").ap()
    gates = nc.dram_tensor("i_gates", [B2, 128, NT * NE], F32, kind="Internal").ap()
    attnT = nc.dram_tensor("i_attnT", [B2, D, S], BF16, kind="Internal").ap()
    sT = nc.dram_tensor("i_sT", [B2, NT, NC, 128, 128], BF16, kind="Internal").ap()
    x1 = nc.dram_tensor("i_x1", [B2, S, D], F32, kind="Internal").ap()
    xtok1 = nc.dram_tensor("i_xtok1", [B2, S, D], BF16, kind="Internal").ap()
    xnT1 = nc.dram_tensor("i_xnT1", [B2, D, S], BF16, kind="Internal").ap()
    s1 = nc.dram_tensor("i_s1", [B2, S, D], BF16, kind="Internal").ap()
    wgT = nc.dram_tensor("i_wgT", [NI, 128, NC, 128], BF16, kind="Internal").ap()
    wuT = nc.dram_tensor("i_wuT", [NI, 128, NC, 128], BF16, kind="Internal").ap()
    wdT = nc.dram_tensor("i_wdT", [NC, 128, NI, 128], BF16, kind="Internal").ap()
    with contextlib.ExitStack() as gx:
        ctx = _Ctx()
        ctx.nc = nc
        for cn in ("cL", "cW", "cV", "cA", "cP", "cS"):
            setattr(ctx, cn, Ctr(gx, nc, cn))
        ctx.block = DeferBlock(gx.enter_context(nc.Block()))
        emit_l0_ln(ctx, x, gw, ident, xnT, gates,
                   wprep_args=(wg_h, wu_h, wd_h, wgT, wuT, wdT))
        barrier(ctx)
        emit_l0_attn(ctx, xnT, wq_h, wk_h, wv_h, ident, identf, attnT)
        barrier(ctx)
        emit_l0_wo(ctx, attnT, wo_h, ident, sT)
        barrier(ctx)
        emit_l0_moe(ctx, x, xnT, sT, gates, expw_h, expb_h, ident, onesb, x1)
        barrier(ctx)
        emit_l1_ln(ctx, x1, ident, xtok1, xnT1)
        barrier(ctx)
        emit_l1_attn(ctx, xtok1, ident, cmask_h, s1)
        barrier(ctx)
        emit_l1_ffn(ctx, x1, xnT1, s1, wgT, wuT, wdT, gsc_h, usc_h, dsc_h, ident, out_h)
        ctx.block.flush()
    return nc


_IDENT_BF = np.eye(128, dtype=BF)


def _shard_x(x):
    return [np.ascontiguousarray(x[c * B2:(c + 1) * B2]) for c in range(NCORE)]


def kernel(**inputs):
    nc = build_fused()
    shards = _shard_x(inputs["L0_attn_hidden_states"])
    cmask = np.triu(np.full((128, 128), -1e9, np.float32), 1)
    ones = np.ones((1, 128), dtype=BF)
    gsc = np.ascontiguousarray(inputs["L1_ffn_gate_scale"].reshape(NI, 128).T)
    usc = np.ascontiguousarray(inputs["L1_ffn_up_scale"].reshape(NI, 128).T)
    dsc = np.ascontiguousarray(inputs["L1_ffn_down_scale"].reshape(NC, 128).T)
    in_maps = [{
        "x": shards[c],
        "wq": inputs["L0_attn_Wq"], "wk": inputs["L0_attn_Wk"],
        "wv": inputs["L0_attn_Wv"], "wo": inputs["L0_attn_Wo"],
        "gw": inputs["L0_ffn_gate_w"],
        "expw": inputs["L0_ffn_expert_w"], "expb": inputs["L0_ffn_expert_b"],
        "l1wg": inputs["L1_ffn_gate_w"], "l1wu": inputs["L1_ffn_up_w"],
        "l1wd": inputs["L1_ffn_down_w"],
        "gsc": gsc, "usc": usc, "dsc": dsc,
        "ident": _IDENT_BF, "identf": np.eye(128, dtype=np.float32),
        "cmask": cmask, "cmaskt": np.ascontiguousarray(cmask.T), "onesb": ones,
    } for c in range(NCORE)]
    res = run_bass_kernel_spmd(nc, in_maps, core_ids=list(range(NCORE)))
    return np.concatenate([r["out"] for r in res.results], axis=0)


# version 24
# speedup vs baseline: 1.0027x; 1.0027x over previous
import contextlib
import numpy as np
import ml_dtypes
import concourse.bass as bass
import concourse.mybir as mybir
from concourse.bass_utils import run_bass_kernel_spmd

F32 = mybir.dt.float32
BF16 = mybir.dt.bfloat16
I32 = mybir.dt.int32
AF = mybir.ActivationFunctionType
ALU = mybir.AluOpType
AX = mybir.AxisListType

B = 16
NCORE = 8
B2 = B // NCORE
S = 2048
D = 2048
NC = D // 128
NT = S // 128
H = 16
HD = 128
SCALE = HD ** -0.5
NE = 8
FI = 8192
NI = FI // 128
EPS = 1e-5
BF = ml_dtypes.bfloat16


class Ctr:
    def __init__(self, stack, nc, name):
        self.sem = stack.enter_context(nc.semaphore(name))
        self.v = 0

    def inc(self, instr, n=1):
        instr.then_inc(self.sem, n)
        self.v += n
        return self.v


def emit_l0_ln(ctx, x, gw, ident, xnT, gates, wprep_args=None):
    nc = ctx.nc
    PFX = 'a_'
    with contextlib.ExitStack() as ex:
        E = ex.enter_context
        xt = [E(nc.sbuf_tensor(PFX + f"xt{p}", [128, D], F32)) for p in range(2)]
        xnf = [E(nc.sbuf_tensor(PFX + f"xnf{p}", [128, D], F32)) for p in range(2)]
        xnb = E(nc.sbuf_tensor(PFX + "xnb", [128, D], BF16))
        xnl = E(nc.sbuf_tensor(PFX + "xnl", [128, D], BF16))
        xnT_sb = E(nc.sbuf_tensor(PFX + "xnT_sb", [128, NC * S], BF16))
        xnloT = E(nc.sbuf_tensor(PFX + "xnloT", [128, NC * 128], BF16))
        id_sb = E(nc.sbuf_tensor(PFX + "id_sb", [128, 128], BF16))
        gw_sb = E(nc.sbuf_tensor(PFX + "gw_sb", [NE, D], F32))
        gwh = E(nc.sbuf_tensor(PFX + "gwh", [NE, D], BF16))
        gwl = E(nc.sbuf_tensor(PFX + "gwl", [NE, D], BF16))
        gwT = E(nc.sbuf_tensor(PFX + "gwT", [128, 2 * NC * NE], BF16))
        mv = E(nc.sbuf_tensor(PFX + "mv", [128, 8], F32))
        lnv = E(nc.sbuf_tensor(PFX + "lnv", [128, 2], F32))
        rstd = E(nc.sbuf_tensor(PFX + "rstd", [128, 2], F32))
        logit = E(nc.sbuf_tensor(PFX + "logit", [128, NT * NE], F32))
        gat = E(nc.sbuf_tensor(PFX + "gat", [128, NT * NE], F32))
        sc1 = E(nc.sbuf_tensor(PFX + "sc1", [128, NT * NE], F32))
        sc2 = E(nc.sbuf_tensor(PFX + "sc2", [128, NT * NE], F32))
        sc3 = E(nc.sbuf_tensor(PFX + "sc3", [128, NT * NE], F32))
        sm = E(nc.sbuf_tensor(PFX + "sm", [128, 5 * NT], F32))
        pb = [E(nc.psum_tensor(PFX + f"pb{k}", [128, 1024], BF16)) for k in range(4)]
        pL = E(nc.psum_tensor(PFX + "pL", [128, 512], F32))
        cL = ctx.cL
        cV = ctx.cV
        cA = ctx.cA
        cP = ctx.cP
        cS = ctx.cS
        block = ctx.block

        @block.sync
        def _(sy):
            sy.dma_start(out=id_sb[:], in_=ident[:, :]).then_inc(cL.sem, 16)
            sy.dma_start(out=gw_sb[:], in_=gw[:, :]).then_inc(cL.sem, 16)
        cL.v += 32
        ld_prep = cL.v
        if wprep_args is not None:
            wprep_step = make_wprep_stepper(ctx, E, id_sb, ld_prep, *wprep_args)
        else:
            wprep_step = lambda k: None

        cA.v += 1
        a_gwh = cA.v
        @block.scalar
        def _(sc):
            sc.wait_ge(cL.sem, ld_prep)
            sc.copy(gwh[:], gw_sb[:]).then_inc(cA.sem, 1)

        @block.vector
        def _(v):
            v.wait_ge(cA.sem, a_gwh)
            cV.inc(v.tensor_tensor(gwl[:], gw_sb[:], gwh[:], op=ALU.subtract))
        v_gwl = cV.v

        @block.tensor
        def _(t):
            t.wait_ge(cV.sem, v_gwl)
            for dc in range(NC):
                t.transpose(pb[0][:, dc * 8:(dc + 1) * 8], gwh[0:NE, dc * 128:(dc + 1) * 128],
                            id_sb[0:NE, 0:NE])
                last = t.transpose(pb[0][:, 128 + dc * 8:128 + (dc + 1) * 8],
                                   gwl[0:NE, dc * 128:(dc + 1) * 128], id_sb[0:NE, 0:NE])
            cP.inc(last)
        p_gwT = cP.v

        @block.vector
        def _(v):
            v.wait_ge(cP.sem, p_gwT)
            cV.inc(v.tensor_copy(gwT[:, 0:256], pb[0][:, 0:256]))
        v_gwT = cV.v

        evac_done = {}
        logit_done = {}
        pv_xn2 = {0: 0, 1: 0}
        for b in range(B2):
            for i in range(NT):
                it = b * NT + i
                par = it % 2
                @block.sync
                def _(sy, b=b, i=i, par=par, it=it):
                    if it >= 2:
                        sy.wait_ge(cV.sem, evac_done[(b, i - 2) if i >= 2 else (b - 1, NT + i - 2)])
                    sy.dma_start(out=xt[par][:], in_=x[b, i * 128:(i + 1) * 128, :]).then_inc(cL.sem, 16)
                cL.v += 16
                ld_here = cL.v

                cA.v += 1
                a_st = cA.v
                @block.scalar
                def _(sc, par=par, ld=ld_here, pvx=pv_xn2[par]):
                    sc.wait_ge(cL.sem, ld)
                    if pvx:
                        sc.wait_ge(cV.sem, pvx)
                    sc.activation(xnf[par][:], xt[par][:], AF.Identity,
                                  accum_out=mv[:, par * 4: par * 4 + 1])
                    sc.activation(xnf[par][:], xt[par][:], AF.Square,
                                  accum_out=mv[:, par * 4 + 3: par * 4 + 4]).then_inc(cA.sem, 1)

                @block.vector
                def _(v, a_st=a_st, o=par * 4):
                    v.wait_ge(cA.sem, a_st)
                    v.tensor_scalar(mv[:, o:o + 1], mv[:, o:o + 1], 1.0 / D, None, op0=ALU.mult)
                    cV.inc(v.tensor_scalar(mv[:, o + 3:o + 4], mv[:, o + 3:o + 4], 1.0 / D, EPS,
                                           op0=ALU.mult, op1=ALU.add))
                    v.wait_ge(cV.sem, cV.v)
                    cV.inc(v.tensor_tensor(mv[:, o + 2:o + 3], mv[:, o:o + 1], mv[:, o:o + 1],
                                           op=ALU.mult))
                    v.wait_ge(cV.sem, cV.v)
                    cV.inc(v.tensor_tensor(mv[:, o + 1:o + 2], mv[:, o + 3:o + 4],
                                           mv[:, o + 2:o + 3], op=ALU.subtract))
                v_bn = cV.v

                cA.v += 2
                a_rstd = cA.v
                @block.scalar
                def _(sc, v_bn=v_bn, a1=a_rstd - 1, par=par, o=par * 4):
                    sc.wait_ge(cV.sem, v_bn)
                    sc.activation(lnv[:, par:par + 1], mv[:, o + 1:o + 2], AF.Ln).then_inc(cA.sem, 1)
                    sc.wait_ge(cA.sem, a1)
                    sc.activation(rstd[:, par:par + 1], lnv[:, par:par + 1], AF.Exp,
                                  scale=-0.5).then_inc(cA.sem, 1)

                @block.vector
                def _(v, par=par, a_rstd=a_rstd, o=par * 4):
                    v.wait_ge(cA.sem, a_rstd)
                    v.tensor_scalar(xnf[par][:], xt[par][:], mv[:, o:o + 1],
                                    rstd[:, par:par + 1], op0=ALU.subtract, op1=ALU.mult)
                    v.tensor_copy(xnb[:], xnf[par][:])
                    cV.inc(v.tensor_tensor(xnl[:], xnf[par][:], xnb[:], op=ALU.subtract))
                v_xn = cV.v
                pv_xn2[par] = v_xn

                @block.tensor
                def _(t, v_xn=v_xn, it=it):
                    t.wait_ge(cV.sem, v_xn)
                    if it > 0:
                        t.wait_ge(cV.sem, evac_done[prev_bi])
                    else:
                        t.wait_ge(cV.sem, v_gwT)
                    for dc in range(NC):
                        bank, sl = divmod(dc, 8)
                        t.transpose(pb[bank][:, sl * 128:(sl + 1) * 128],
                                    xnb[:, dc * 128:(dc + 1) * 128], id_sb[:])
                        last = t.transpose(pb[2 + bank][:, sl * 128:(sl + 1) * 128],
                                           xnl[:, dc * 128:(dc + 1) * 128], id_sb[:])
                    cP.inc(last)
                p_tr = cP.v

                @block.vector
                def _(v, i=i, p_tr=p_tr, it=it):
                    v.wait_ge(cP.sem, p_tr)
                    if it > 0:
                        v.wait_ge(cP.sem, logit_done[prev_bi])
                    for bank in range(2):
                        v.tensor_copy(
                            xnT_sb[:].rearrange("p (c t) -> p c t", c=NC)[:, bank * 8:(bank + 1) * 8,
                                                                          i * 128:(i + 1) * 128],
                            pb[bank][:].rearrange("p (c t) -> p c t", c=8))
                    for bank in range(2):
                        last = v.tensor_copy(
                            xnloT[:].rearrange("p (c t) -> p c t", c=NC)[:, bank * 8:(bank + 1) * 8, :],
                            pb[2 + bank][:].rearrange("p (c t) -> p c t", c=8))
                    cV.inc(last)
                evac_done[(b, i)] = cV.v

                @block.tensor
                def _(t, b=b, i=i, ev=evac_done[(b, i)]):
                    t.wait_ge(cV.sem, ev)
                    t.wait_ge(cV.sem, v_gwT)
                    n = 0
                    for dc in range(NC):
                        hi_t = xnT_sb[:, dc * S + i * 128: dc * S + (i + 1) * 128]
                        lo_t = xnloT[:, dc * 128:(dc + 1) * 128]
                        for lhs, rhs in ((hi_t, gwT[:, dc * 8:(dc + 1) * 8]),
                                         (lo_t, gwT[:, dc * 8:(dc + 1) * 8]),
                                         (hi_t, gwT[:, 128 + dc * 8:128 + (dc + 1) * 8])):
                            last = t.matmul(pL[:, 0:8], lhs, rhs, start=(n == 0), stop=(n == 3 * NC - 1))
                            n += 1
                    cP.inc(last)
                logit_done[(b, i)] = cP.v

                @block.vector
                def _(v, b=b, i=i, pl=logit_done[(b, i)]):
                    v.wait_ge(cP.sem, pl)
                    cV.inc(v.tensor_copy(logit[:, i * 8:(i + 1) * 8], pL[:, 0:8]))
                prev_bi = (b, i)
                wprep_step(5)

            def c3(t):
                return t[:].rearrange("p (c e) -> p c e", e=NE)

            def b3(t, off):
                return (t[:, off:off + NT].rearrange("p (c e) -> p c e", e=1)
                        .broadcast_to([128, NT, NE]))

            m1s, m2s, dds, w1s, w2s = 0, NT, 2 * NT, 3 * NT, 4 * NT

            @block.vector
            def _(v):
                cV.inc(v.tensor_reduce(sm[:, m1s:m1s + NT], c3(logit), axis=AX.X, op=ALU.max))
                v.wait_ge(cV.sem, cV.v)
                v.tensor_tensor(c3(sc1), c3(logit), b3(sm, m1s), op=ALU.is_ge)
                cV.inc(v.scalar_tensor_tensor(c3(sc3), c3(sc1), -1e30, c3(logit),
                                              op0=ALU.mult, op1=ALU.add))
                v.wait_ge(cV.sem, cV.v)
                cV.inc(v.tensor_reduce(sm[:, m2s:m2s + NT], c3(sc3), axis=AX.X, op=ALU.max))
                v.wait_ge(cV.sem, cV.v)
                v.tensor_tensor(c3(sc2), c3(sc3), b3(sm, m2s), op=ALU.is_ge)
                cV.inc(v.tensor_tensor(sm[:, dds:dds + NT], sm[:, m2s:m2s + NT],
                                       sm[:, m1s:m1s + NT], op=ALU.subtract))
            v_dd = cV.v

            cA.v += 1
            a_e = cA.v
            @block.scalar
            def _(sc, v_dd=v_dd):
                sc.wait_ge(cV.sem, v_dd)
                sc.activation(sm[:, dds:dds + NT], sm[:, dds:dds + NT], AF.Exp).then_inc(cA.sem, 1)

            @block.vector
            def _(v, a_e=a_e):
                v.wait_ge(cA.sem, a_e)
                cV.inc(v.tensor_scalar(sm[:, dds:dds + NT], sm[:, dds:dds + NT], 1.0, None, op0=ALU.add))
                v.wait_ge(cV.sem, cV.v)
                cV.inc(v.reciprocal(sm[:, w1s:w1s + NT], sm[:, dds:dds + NT]))
                v.wait_ge(cV.sem, cV.v)
                cV.inc(v.tensor_scalar(sm[:, w2s:w2s + NT], sm[:, w1s:w1s + NT], -1.0, 1.0,
                                       op0=ALU.mult, op1=ALU.add))
                v.wait_ge(cV.sem, cV.v)
                cV.inc(v.tensor_tensor(c3(gat), c3(sc1), b3(sm, w1s), op=ALU.mult))
                v.wait_ge(cV.sem, cV.v)
                cV.inc(v.tensor_tensor(c3(sc2), c3(sc2), b3(sm, w2s), op=ALU.mult))
                v.wait_ge(cV.sem, cV.v)
                cV.inc(v.tensor_tensor(gat[:], gat[:], sc2[:], op=ALU.add))
            v_gates = cV.v


            @block.sync
            def _(sy, b=b, v_gates=v_gates, ev=evac_done[(b, NT - 1)]):
                sy.wait_ge(cV.sem, max(v_gates, ev))
                for dc in range(NC):
                    sy.dma_start(out=xnT[b, dc * 128:(dc + 1) * 128, :],
                                 in_=xnT_sb[:, dc * S:(dc + 1) * S]).then_inc(cS.sem, 16)
                sy.dma_start(out=gates[b, :, :], in_=gat[:, :]).then_inc(cS.sem, 16)
            cS.v += 17 * 16
            st_b = cS.v
            if b + 1 < B2:
                @block.tensor
                def _(t, st_b=st_b):
                    t.wait_ge(cS.sem, st_b)
                @block.vector
                def _(v, st_b=st_b):
                    v.wait_ge(cS.sem, st_b)

        @block.sync
        def _(sy):
            sy.wait_ge(cS.sem, cS.v)


def emit_l0_attn(ctx, xnT_h, wq_h, wk_h, wv_h, ident, identf, attnT):
    nc = ctx.nc
    PFX = 'b_'
    with contextlib.ExitStack() as ex:
        E = ex.enter_context
        xnb_sb = E(nc.sbuf_tensor(PFX + "xnb_sb", [128, NC * S], BF16))
        id_sb = E(nc.sbuf_tensor(PFX + "id_sb", [128, 128], BF16))
        id32_sb = E(nc.sbuf_tensor(PFX + "id32_sb", [128, 128], F32))
        onesc = E(nc.sbuf_tensor(PFX + "onesc", [128, 1], BF16))
        onesr = E(nc.sbuf_tensor(PFX + "onesr", [1, 128], BF16))
        w_sb = [[E(nc.sbuf_tensor(PFX + f"w_sb{p}{j}", [128, D], BF16)) for j in range(3)] for p in range(2)]
        wT = [E(nc.sbuf_tensor(PFX + f"wT{j}", [128, NC * 128], BF16)) for j in range(3)]
        qT = E(nc.sbuf_tensor(PFX + "qT", [128, S], BF16))
        kT = E(nc.sbuf_tensor(PFX + "kT", [128, S], BF16))
        vT = E(nc.sbuf_tensor(PFX + "vT", [128, S], BF16))
        vtok = E(nc.sbuf_tensor(PFX + "vtok", [128, NT * 128], BF16))
        pT_sb = E(nc.sbuf_tensor(PFX + "pT_sb", [128, NT * 512], BF16))
        row_sb = [E(nc.sbuf_tensor(PFX + f"row_sb{p}", [128, S], BF16)) for p in range(2)]
        mx = E(nc.sbuf_tensor(PFX + "mx", [128, 3], F32))
        ngrow = [E(nc.sbuf_tensor(PFX + f"ngrow{p}", [1, 512], BF16)) for p in range(2)]
        ngmin = E(nc.sbuf_tensor(PFX + "ngmin", [1, 2], BF16))
        negC = E(nc.sbuf_tensor(PFX + "negC", [128, 2], F32))
        onesq = E(nc.sbuf_tensor(PFX + "onesq", [128, 128], BF16))
        Rf = E(nc.sbuf_tensor(PFX + "Rf", [128, 512], F32))
        pA = E(nc.psum_tensor(PFX + "pA", [128, 1024], F32))
        pST = [E(nc.psum_tensor(PFX + f"pST{k}", [128, 512], F32)) for k in range(3)]
        pSum = E(nc.psum_tensor(PFX + "pSum", [128, 512], F32))
        pO = E(nc.psum_tensor(PFX + "pO", [128, 512], F32))
        pNg = E(nc.psum_tensor(PFX + "pNg", [128, 512], F32))
        cL, cW, cV, cA, cP, cS = ctx.cL, ctx.cW, ctx.cV, ctx.cA, ctx.cP, ctx.cS
        block = ctx.block

        @block.sync
        def _(sy):
            sy.dma_start(out=id_sb[:], in_=ident[:, :]).then_inc(cL.sem, 16)
            sy.dma_start(out=id32_sb[:], in_=identf[:, :]).then_inc(cL.sem, 16)
        cL.v += 32

        @block.vector
        def _(v):
            v.memset(onesc[:], 1.0)
            v.memset(onesr[:], 1.0)
            cV.inc(v.memset(onesq[:], 1.0))
        ones_ready = cV.v

        wprep_done = {}
        row_store = {}
        def issue_wload(hh):
            par = hh % 2
            h = hh % H
            @block.gpsimd
            def _(g, par=par, h=h, hh=hh):
                if hh >= 2:
                    g.wait_ge(cP.sem, wprep_done[hh - 2])
                for j, w in enumerate((wq_h, wk_h, wv_h)):
                    g.dma_start(out=w_sb[par][j][:],
                                in_=w[h * 128:(h + 1) * 128, :]).then_inc(cW.sem, 16)
            cW.v += 48
            return cW.v

        wload_done = {0: issue_wload(0), 1: issue_wload(1)}
        pj_busy = [None]
        prev_hd = [0, 0, 0]

        for b in range(B2):
            @block.sync
            def _(sy, b=b, xn_free=cP.v):
                if b > 0:
                    sy.wait_ge(cP.sem, xn_free)
                for dc in range(NC):
                    sy.dma_start(out=xnb_sb[:, dc * S:(dc + 1) * S],
                                 in_=xnT_h[b, dc * 128:(dc + 1) * 128, :]).then_inc(cL.sem, 16)
            cL.v += 16 * NC
            ld_xn = cL.v

            for h in range(H):
                hh = b * H + h
                par = hh % 2

                pa_bf = pA[:].bitcast(BF16)
                wT_ev = 0
                for j in range(3):
                    @block.tensor
                    def _(t, j=j, hh=hh, ph=prev_hd[0], pe=wT_ev):
                        if j == 0:
                            t.wait_ge(cW.sem, wload_done[hh])
                            if ph:
                                t.wait_ge(cV.sem, ph)
                        else:
                            t.wait_ge(cV.sem, pe)
                        for dc in range(NC):
                            last = t.transpose(pa_bf[:, dc * 128:(dc + 1) * 128],
                                               w_sb[par][j][0:128, dc * 128:(dc + 1) * 128],
                                               id_sb[:])
                        cP.inc(last)
                    tr = cP.v

                    @block.vector
                    def _(v, j=j, tr=tr):
                        v.wait_ge(cP.sem, tr)
                        cV.inc(v.tensor_copy(
                            wT[j][:].rearrange("p (c t) -> p c t", t=128),
                            pA[:].bitcast(BF16).rearrange("p (c t) -> p c t", t=128)))
                    wT_ev = cV.v
                wprep_done[hh] = cP.v
                wT_ready = cV.v
                if hh + 2 < B2 * H:
                    wload_done[hh + 2] = issue_wload(hh + 2)

                pjb = [pA[:, 0:512], pA[:, 512:1024], pST[0][:], pST[1][:]]
                for j, dstT in enumerate((qT, kT, vT)):
                    @block.tensor
                    def _(t, j=j, wT_ready=wT_ready, ld_xn=ld_xn, pj=pj_busy[0]):
                        t.wait_ge(cV.sem, wT_ready)
                        t.wait_ge(cL.sem, ld_xn)
                        if pj is not None:
                            t.wait_ge(pj[0].sem, pj[1])
                        for dc in range(NC):
                            for n in range(4):
                                last = t.matmul(pjb[n], wT[j][:, dc * 128:(dc + 1) * 128],
                                                xnb_sb[:, dc * S + n * 512: dc * S + n * 512 + 512],
                                                start=(dc == 0), stop=(dc == NC - 1))
                        cP.inc(last)
                    pj_mm = cP.v

                    @block.vector
                    def _(v, j=j, dstT=dstT, pj_mm=pj_mm):
                        v.wait_ge(cP.sem, pj_mm)
                        for n in range(4):
                            if j == 0:
                                last = v.tensor_scalar(dstT[:, n * 512:(n + 1) * 512], pjb[n],
                                                       float(SCALE), None, op0=ALU.mult)
                            else:
                                last = v.tensor_copy(dstT[:, n * 512:(n + 1) * 512], pjb[n])
                        cV.inc(last)
                    pj_busy[0] = (cV, cV.v)
                proj_done_v = cV.v

                @block.tensor
                def _(t, proj_done_v=proj_done_v):
                    t.wait_ge(cV.sem, proj_done_v)
                    for tc in range(NT):
                        last = t.transpose(pA[:].bitcast(BF16)[:, tc * 128:(tc + 1) * 128],
                                           vT[0:128, tc * 128:(tc + 1) * 128], id_sb[:])
                    cP.inc(last)
                vt_tr = cP.v

                @block.vector
                def _(v, vt_tr=vt_tr):
                    v.wait_ge(cP.sem, vt_tr)
                    cV.inc(v.tensor_copy(
                        vtok[:].rearrange("p (c t) -> p c t", t=128),
                        pA[:].bitcast(BF16).rearrange("p (c t) -> p c t", t=128)))
                vtok_ready = cV.v

                max_hist = {}
                ng_ev = {}
                exp_hist = {}
                p2_last = {}
                p2_tt = {}
                ngtr_hist = {}
                negc_ev = {}

                red_hist = {}

                def emit_pass1_half(qblk, sub, half):
                    qi = qblk * 4 + sub
                    @block.tensor
                    def _(t, qi=qi, half=half,
                          mh=(max_hist.get(qi - 1, 0) if half == 0 else red_hist[qi]),
                          vr=vtok_ready):
                        if half == 0 and qi % 4 == 0:
                            t.wait_ge(cV.sem, vr)
                        if mh:
                            t.wait_ge(cV.sem, mh)
                        for n in range(2):
                            last = t.matmul(pA[:, n * 512:(n + 1) * 512],
                                            qT[:, qi * 128:(qi + 1) * 128],
                                            kT[:, half * 1024 + n * 512: half * 1024 + n * 512 + 512],
                                            start=True, stop=True)
                        cP.inc(last)
                    s_done = cP.v

                    @block.vector
                    def _(v, half=half, s_done=s_done):
                        v.wait_ge(cP.sem, s_done)
                        cV.inc(v.tensor_reduce(mx[:, half:half + 1], pA[:], axis=AX.X,
                                               op=ALU.max, negate=True))
                    red_hist[qi] = cV.v
                    if half == 1:
                        @block.vector
                        def _(v, rv=cV.v):
                            v.wait_ge(cV.sem, rv)
                            cV.inc(v.tensor_tensor(mx[:, 2:3], mx[:, 0:1], mx[:, 1:2],
                                                   op=ALU.min))
                            v.wait_ge(cV.sem, cV.v)
                        max_hist[qi] = cV.v

                def emit_ngtr(qblk, sub):
                    qi = qblk * 4 + sub
                    @block.tensor
                    def _(t, qi=qi, sub=sub, mh=max_hist[qi],
                          ngv=negc_ev.get(qblk - 1, 0)):
                        t.wait_ge(cV.sem, mh)
                        if sub == 0 and ngv:
                            t.wait_ge(cV.sem, ngv)
                        cP.inc(t.transpose(pNg[0:1, sub * 128:(sub + 1) * 128],
                                           mx[:, 2:3], id32_sb[:]))
                    ngtr_hist[qblk] = cP.v

                def emit_ng_evac(qblk):
                    @block.vector
                    def _(v, qblk=qblk, ngt=ngtr_hist[qblk], par2=qblk % 2,
                          p2=p2_last.get(qblk - 2, 0)):
                        v.wait_ge(cP.sem, ngt)
                        if p2:
                            v.wait_ge(cP.sem, p2)
                        cV.inc(v.tensor_copy(ngrow[par2][:], pNg[0:1, 0:512]))
                        v.wait_ge(cV.sem, cV.v)
                        cV.inc(v.tensor_reduce(ngmin[0:1, par2:par2 + 1], ngrow[par2][:],
                                               axis=AX.X, op=ALU.min))
                        v.wait_ge(cV.sem, cV.v)
                    ng_ev[qblk] = cV.v

                    @block.tensor
                    def _(t, qblk=qblk, ngv=ng_ev[qblk], par2=qblk % 2,
                          pe=negc_ev.get(qblk - 1, 0)):
                        t.wait_ge(cV.sem, ngv)
                        if pe:
                            t.wait_ge(cV.sem, pe)
                        cP.inc(t.matmul(pNg[:, 0:1], onesr[0:1, :], ngmin[0:1, par2:par2 + 1],
                                        start=True, stop=True))
                    bc_mm = cP.v

                    @block.vector
                    def _(v, qblk=qblk, bc_mm=bc_mm, par2=qblk % 2):
                        v.wait_ge(cP.sem, bc_mm)
                        cV.inc(v.tensor_copy(negC[:, par2:par2 + 1], pNg[:, 0:1]))
                    negc_ev[qblk] = cV.v

                def emit_pass2_kcs(qblk, kcs):
                    par2 = qblk % 2
                    for kc in kcs:
                        stp = kc % 3
                        @block.tensor
                        def _(t, qblk=qblk, kc=kc, stp=stp, par2=par2,
                              ngv=ng_ev[qblk], ex=exp_hist.get((qblk, kc - 3), 0),
                              px=(p2_last.get(qblk - 1, 0) if kc == 0 else 0),
                              ones_ready=ones_ready):
                            if kc == 0:
                                t.wait_ge(cV.sem, max(ngv, ones_ready))
                                if px:
                                    t.wait_ge(cP.sem, px)
                            if ex:
                                t.wait_ge(cA.sem, ex)
                            last = t.matmul(pST[stp][:], kT[:, kc * 128:(kc + 1) * 128],
                                            qT[:, qblk * 512:(qblk + 1) * 512],
                                            start=True, stop=True)
                            cP.inc(last)
                        st_mm = cP.v

                        cA.v += 1
                        exp_hist[(qblk, kc)] = cA.v
                        @block.scalar
                        def _(sc, qblk=qblk, kc=kc, stp=stp, st_mm=st_mm, par2=par2,
                              ngc=negc_ev[qblk]):
                            sc.wait_ge(cP.sem, st_mm)
                            if kc == 0:
                                sc.wait_ge(cV.sem, ngc)
                            sc.activation(pT_sb[:, kc * 512:(kc + 1) * 512], pST[stp][:],
                                          AF.Exp, bias=negC[:, par2:par2 + 1]).then_inc(cA.sem, 1)

                        if kc >= 1:
                            emit_sumpv(qblk, kc - 1)
                    if kcs[-1] == NT - 1:
                        emit_sumpv(qblk, NT - 1)
                        p2_last[qblk] = cP.v

                def emit_sumpv(qblk, kc):
                    @block.tensor
                    def _(t, qblk=qblk, kc=kc, ex=exp_hist[(qblk, kc)],
                          pv=(p2_tt.get(qblk - 1, 0) if kc == 0 else 0)):
                        t.wait_ge(cA.sem, ex)
                        if pv:
                            t.wait_ge(cV.sem, pv)
                        t.matmul(pSum[:], onesq[:, :],
                                 pT_sb[:, kc * 512:(kc + 1) * 512],
                                 start=(kc == 0), stop=(kc == NT - 1))
                        last = t.matmul(pO[:], vtok[:, kc * 128:(kc + 1) * 128],
                                        pT_sb[:, kc * 512:(kc + 1) * 512],
                                        start=(kc == 0), stop=(kc == NT - 1))
                        cP.inc(last)

                def emit_pass2_tail(qblk):
                    @block.vector
                    def _(v, qblk=qblk, pl=p2_last[qblk], par=par, rs=row_store.get(par, 0)):
                        v.wait_ge(cP.sem, pl)
                        if qblk == 0 and rs:
                            v.wait_ge(cS.sem, rs)
                        cV.inc(v.reciprocal(Rf[:], pSum[:]))
                        v.wait_ge(cV.sem, cV.v)
                        cV.inc(v.tensor_tensor(row_sb[par][:, qblk * 512:(qblk + 1) * 512],
                                               pO[:], Rf[:], op=ALU.mult))
                    p2_tt[qblk] = cV.v

                for qblk in range(5):
                    for sub in range(4):
                        if qblk < 4:
                            emit_pass1_half(qblk, sub, 0)
                        if qblk >= 1:
                            emit_pass2_kcs(qblk - 1, [sub * 4, sub * 4 + 1])
                        if qblk < 4:
                            emit_pass1_half(qblk, sub, 1)
                        if qblk >= 1:
                            emit_pass2_kcs(qblk - 1, [sub * 4 + 2, sub * 4 + 3])
                        if qblk < 4:
                            emit_ngtr(qblk, sub)
                    if qblk < 4:
                        emit_ng_evac(qblk)
                    if qblk >= 1:
                        emit_pass2_tail(qblk - 1)

                prev_hd[0] = max_hist[NT - 1]

                @block.sync
                def _(sy, b=b, h=h, par=par, oe=p2_tt[3]):
                    sy.wait_ge(cV.sem, oe)
                    sy.dma_start(out=attnT[b, h * 128:(h + 1) * 128, :],
                                 in_=row_sb[par][:]).then_inc(cS.sem, 16)
                cS.v += 16
                row_store[par] = cS.v

        @block.sync
        def _(sy):
            sy.wait_ge(cS.sem, cS.v)

def emit_l0_wo(ctx, attnT_h, wo_h, ident, sT):
    nc = ctx.nc
    PFX = 'c_'
    with contextlib.ExitStack() as ex:
        E = ex.enter_context
        id_sb = E(nc.sbuf_tensor(PFX + "id_sb", [128, 128], BF16))
        wo_sb = [E(nc.sbuf_tensor(PFX + f"wo_sb{p}", [128, D], BF16)) for p in range(2)]
        wo_f = [E(nc.sbuf_tensor(PFX + f"wo_f{p}", [128, D], F32)) for p in range(2)]
        woT = E(nc.sbuf_tensor(PFX + "woT", [128, NC * D], BF16))
        at_sb = [E(nc.sbuf_tensor(PFX + f"at_sb{p}", [128, NC * 512], BF16)) for p in range(2)]
        st_sb = [E(nc.sbuf_tensor(PFX + f"st_sb{p}", [128, 512], BF16)) for p in range(2)]
        st2_sb = [E(nc.sbuf_tensor(PFX + f"st2_sb{p}", [128, 512], BF16)) for p in range(2)]
        pT0 = E(nc.psum_tensor(PFX + "pT0", [128, 1024], BF16))
        pT1 = E(nc.psum_tensor(PFX + "pT1", [128, 1024], BF16))
        pPJ = [E(nc.psum_tensor(PFX + f"pPJ{p}", [128, 512], F32)) for p in range(2)]
        cL = ctx.cL
        cW = ctx.cW
        cV = ctx.cV
        cP = ctx.cP
        cS = ctx.cS
        block = ctx.block

        @block.sync
        def _(sy):
            sy.dma_start(out=id_sb[:], in_=ident[:, :]).then_inc(cL.sem, 16)
        cL.v += 16

        pT = [pT0, pT1]
        woT_done = 0
        conv_done = {0: 0, 1: 0}
        for oc in range(NC):
            par = oc % 2
            @block.sync
            def _(sy, oc=oc, par=par, cv=conv_done[par]):
                if oc >= 2:
                    sy.wait_ge(cV.sem, cv)
                sy.dma_start(out=wo_f[par][:],
                             in_=wo_h[oc * 128:(oc + 1) * 128, :]).then_inc(cW.sem, 16)
            cW.v += 16
            wl_raw = cW.v

            @block.vector
            def _(v, oc=oc, par=par, wl_raw=wl_raw, prev=woT_done):
                v.wait_ge(cW.sem, wl_raw)
                if oc >= 2:
                    v.wait_ge(cP.sem, prev)
                cV.inc(v.tensor_copy(wo_sb[par][:], wo_f[par][:]))
            conv_done[par] = cV.v
            wl = (cV, cV.v)
            for half in range(2):
                bk = half
                @block.tensor
                def _(t, oc=oc, par=par, half=half, bk=bk, wl=wl):
                    t.wait_ge(wl[0].sem, wl[1])
                    if oc >= 1:
                        t.wait_ge(cV.sem, ev_done[bk])
                    elif half == 0:
                        t.wait_ge(cL.sem, 16)
                    for s_idx in range(8):
                        dc = half * 8 + s_idx
                        last = t.transpose(pT[bk][:, s_idx * 128:(s_idx + 1) * 128],
                                           wo_sb[par][0:128, dc * 128:(dc + 1) * 128], id_sb[:])
                    cP.inc(last)
                tr = cP.v

                @block.vector
                def _(v, oc=oc, half=half, bk=bk, tr=tr):
                    v.wait_ge(cP.sem, tr)
                    cV.inc(v.tensor_copy(
                        woT[:].rearrange("p (c t) -> p c t", t=D)[:, half * 8:half * 8 + 8,
                                                                  oc * 128:(oc + 1) * 128],
                        pT[bk][:].rearrange("p (c t) -> p c t", t=128)))
                if oc == 0 and half == 0:
                    ev_done = {0: 0, 1: 0}
                ev_done[bk] = cV.v
            woT_done = cP.v
        woT_ready = cV.v

        st_free = {0: 0, 1: 0}
        tr_free = {0: 0, 1: 0}
        at_free = {0: 0, 1: 0}
        pj_free = {0: 0, 1: 0}
        pj_free_eng = {0: None, 1: None}
        for b in range(B2):
            for n in range(4):
                nb = (b * 4 + n) % 2
                @block.sync
                def _(sy, b=b, n=n, nb=nb):
                    if at_free[nb]:
                        sy.wait_ge(cP.sem, at_free[nb])
                    for dc in range(NC):
                        sy.dma_start(out=at_sb[nb][:, dc * 512:(dc + 1) * 512],
                                     in_=attnT_h[b, dc * 128:(dc + 1) * 128,
                                                 n * 512:(n + 1) * 512]).then_inc(cL.sem, 16)
                cL.v += 16 * NC
                at_ld = cL.v

                for oc in range(NC):
                    pj = oc % 2
                    @block.tensor
                    def _(t, oc=oc, nb=nb, pj=pj, at_ld=at_ld, wr=woT_ready):
                        t.wait_ge(cL.sem, at_ld)
                        t.wait_ge(cV.sem, wr)
                        if pj_free_eng[pj] is not None:
                            eng, val = pj_free_eng[pj]
                            t.wait_ge(eng.sem, val)
                        for dc in range(NC):
                            last = t.matmul(pPJ[pj][:], woT[:, dc * D + oc * 128: dc * D + (oc + 1) * 128],
                                            at_sb[nb][:, dc * 512:(dc + 1) * 512],
                                            start=(dc == 0), stop=(dc == NC - 1))
                        cP.inc(last)
                    mm_done = cP.v
                    if oc == NC - 1:
                        at_free[nb] = mm_done

                    sp = oc % 2
                    @block.vector
                    def _(v, sp=sp, pj=pj, mm_done=mm_done, tfree=tr_free[oc % 2]):
                        v.wait_ge(cP.sem, mm_done)
                        if tfree:
                            v.wait_ge(cP.sem, tfree)
                        cV.inc(v.tensor_copy(st_sb[sp][:], pPJ[pj][:]))
                    pj_free_eng[pj] = (cV, cV.v)
                    ev = cV.v

                    bk = oc % 2
                    @block.tensor
                    def _(t, sp=sp, bk=bk, ev=ev, efree=ev_done[bk]):
                        t.wait_ge(cV.sem, ev)
                        if efree:
                            t.wait_ge(cV.sem, efree)
                        for i in range(4):
                            last = t.transpose(pT[bk][:, i * 128:(i + 1) * 128],
                                               st_sb[sp][0:128, i * 128:(i + 1) * 128], id_sb[:])
                        cP.inc(last)
                    tr_done = cP.v
                    tr_free[sp] = tr_done

                    @block.vector
                    def _(v, sp=sp, bk=bk, tr_done=tr_done, sfree=st_free[sp]):
                        v.wait_ge(cP.sem, tr_done)
                        if sfree:
                            v.wait_ge(cS.sem, sfree)
                        cV.inc(v.tensor_copy(st2_sb[sp][:], pT[bk][:, 0:512]))
                    ev_done[bk] = cV.v
                    ev2 = cV.v

                    @block.sync
                    def _(sy, b=b, n=n, oc=oc, sp=sp, ev2=ev2):
                        sy.wait_ge(cV.sem, ev2)
                        for i in range(4):
                            sy.dma_start(out=sT[b, n * 4 + i, oc, :, :],
                                         in_=st2_sb[sp][:, i * 128:(i + 1) * 128]).then_inc(cS.sem, 16)
                    cS.v += 64
                    st_free[sp] = cS.v

        @block.sync
        def _(sy):
            sy.wait_ge(cS.sem, cS.v)


def emit_l0_moe(ctx, x_h, xnT_h, sT_h, gat_h, expw_h, expb_h, ident, onesb, x1_h):
    nc = ctx.nc
    PFX = 'd_'
    with contextlib.ExitStack() as ex:
        E = ex.enter_context
        id_sb = E(nc.sbuf_tensor(PFX + "id_sb", [128, 128], BF16))
        ones_sb = E(nc.sbuf_tensor(PFX + "ones_sb", [1, 128], BF16))
        eb_sb = E(nc.sbuf_tensor(PFX + "eb_sb", [1, NE * D], BF16))
        xh_sb = E(nc.sbuf_tensor(PFX + "xh_sb", [128, NC * 1024], BF16))
        gat_sb = E(nc.sbuf_tensor(PFX + "gat_sb", [128, NT * NE], F32))
        acc = E(nc.sbuf_tensor(PFX + "acc", [128, 8 * D], BF16))
        wE = [E(nc.sbuf_tensor(PFX + f"wE{p}", [128, NC * 1024], BF16)) for p in range(2)]
        stok = E(nc.sbuf_tensor(PFX + "stok", [128, D], BF16))
        g_sb = E(nc.sbuf_tensor(PFX + "g_sb", [128, D], F32))
        xt = [E(nc.sbuf_tensor(PFX + f"xt{p}", [128, D], F32)) for p in range(2)]
        prod = E(nc.sbuf_tensor(PFX + "prod", [128, D], F32))
        pE = [E(nc.psum_tensor(PFX + f"pE{k}", [128, 512], F32)) for k in range(8)]
        cL = ctx.cL
        cW = ctx.cW
        cV = ctx.cV
        cA = ctx.cA
        cP = ctx.cP
        cS = ctx.cS
        block = ctx.block

        @block.sync
        def _(sy):
            sy.dma_start(out=id_sb[:], in_=ident[:, :]).then_inc(cL.sem, 16)
            sy.dma_start(out=ones_sb[:], in_=onesb[:, :]).then_inc(cL.sem, 16)
        cL.v += 32

        @block.gpsimd
        def _(g):
            g.dma_start(out=eb_sb[:], in_=expb_h.rearrange("e d -> (e d)").unsqueeze(0)).then_inc(cW.sem, 16)
        cW.v += 16
        prep_ld = (cL.v, cW.v)

        wE_free = {0: 0, 1: 0}
        grp_evac = {0: 0, 1: 0}
        comb_store = 0
        sil_done = 0
        last_sil = 0
        pending_wld = {}
        xh_ld = {}

        def emit_gat_load(b, pev):
            @block.sync
            def _(sy, b=b, pev=pev):
                if b > 0:
                    sy.wait_ge(cV.sem, pev)
                sy.dma_start(out=gat_sb[:], in_=gat_h[b, :, :]).then_inc(cL.sem, 16)
            cL.v += 16

        def emit_xh_load(g, pmm):
            gb, ghf = divmod(g, 2)
            @block.sync
            def _(sy, b=gb, hf=ghf, pmm=pmm, g=g):
                if g:
                    sy.wait_ge(cP.sem, pmm)
                for dc in range(NC):
                    sy.dma_start(out=xh_sb[:, dc * 1024:(dc + 1) * 1024],
                                 in_=xnT_h[b, dc * 128:(dc + 1) * 128,
                                           hf * 1024:(hf + 1) * 1024]).then_inc(cL.sem, 16)
            cL.v += 16 * NC
            xh_ld[g] = cL.v

        def emit_wE_load(g, e, ogp):
            eo = g * 16 + e * 2 + ogp
            par = eo % 2
            @block.gpsimd
            def _(gq, e=e, ogp=ogp, par=par, wf=wE_free[par]):
                if wf:
                    gq.wait_ge(cP.sem, wf)
                gq.dma_start(
                    out=wE[par][:].rearrange("p (c o) -> p c o", o=1024),
                    in_=expw_h[e, :, ogp * 1024:(ogp + 1) * 1024]
                    .rearrange("(c p) o -> p c o", p=128)).then_inc(cW.sem, 16)
            cW.v += 16
            pending_wld[(g, e, ogp)] = cW.v

        emit_gat_load(0, 0)
        emit_xh_load(0, 0)
        for g in range(2 * B2):
            b, hf = divmod(g, 2)
            ld_xh = xh_ld[g]
            sil_gate = last_sil if g else 0
            first_evac = [True]
            for e in range(NE):
                for ogp in range(2):
                    eo = g * 16 + e * 2 + ogp
                    par = eo % 2
                    if (g, e, ogp) not in pending_wld:
                        emit_wE_load(g, e, ogp)
                    w_ld = pending_wld[(g, e, ogp)]

                    for tq in range(4):
                        grp = tq % 2
                        @block.tensor
                        def _(t, par=par, e=e, ogp=ogp, tq=tq, grp=grp, w_ld=w_ld,
                              ld_xh=ld_xh, ev=grp_evac[grp]):
                            t.wait_ge(cW.sem, w_ld)
                            t.wait_ge(cL.sem, ld_xh)
                            if ev:
                                t.wait_ge(cV.sem, ev)
                            for dc in range(NC):
                                for tc2 in range(2):
                                    tl = tq * 2 + tc2
                                    for og2 in range(2):
                                        t.matmul(pE[grp * 4 + tc2 * 2 + og2][:],
                                                 xh_sb[:, dc * 1024 + tl * 128: dc * 1024 + (tl + 1) * 128],
                                                 wE[par][:, dc * 1024 + og2 * 512: dc * 1024 + (og2 + 1) * 512],
                                                 start=(dc == 0), stop=False)
                            for tc2 in range(2):
                                for og2 in range(2):
                                    og = ogp * 2 + og2
                                    last = t.matmul(pE[grp * 4 + tc2 * 2 + og2][:], ones_sb[0:1, :],
                                                    eb_sb[0:1, e * D + og * 512: e * D + (og + 1) * 512],
                                                    start=False, stop=True)
                            cP.inc(last)
                        mm_done = cP.v

                        @block.vector
                        def _(v, hf=hf, e=e, ogp=ogp, tq=tq, grp=grp, mm_done=mm_done,
                              fe=first_evac[0], sg=sil_gate):
                            v.wait_ge(cP.sem, mm_done)
                            if fe and sg:
                                v.wait_ge(cA.sem, sg)
                            for tc2 in range(2):
                                for og2 in range(2):
                                    og = ogp * 2 + og2
                                    tl = tq * 2 + tc2
                                    tg = hf * 8 + tl
                                    gcol = gat_sb[:, tg * NE + e: tg * NE + e + 1]
                                    dst = acc[:, tl * D + og * 512: tl * D + (og + 1) * 512]
                                    if e == 0:
                                        last = v.tensor_scalar(dst, pE[grp * 4 + tc2 * 2 + og2][:], gcol,
                                                               None, op0=ALU.mult)
                                    else:
                                        last = v.scalar_tensor_tensor(dst, pE[grp * 4 + tc2 * 2 + og2][:],
                                                                      gcol, dst,
                                                                      op0=ALU.mult, op1=ALU.add)
                            cV.inc(last)
                        grp_evac[grp] = cV.v
                        first_evac[0] = False
                    wE_free[par] = cP.v

            expert_evacs_done = cV.v
            if g + 1 < 2 * B2:
                nb, nhf = divmod(g + 1, 2)
                if nhf == 0:
                    emit_gat_load(nb, cV.v)
                emit_xh_load(g + 1, cP.v)
                emit_wE_load(g + 1, 0, 0)
                emit_wE_load(g + 1, 0, 1)

            for tc in range(8):
                tg = hf * 8 + tc
                par = tc % 2
                @block.sync
                def _(sy, b=b, tg=tg, par=par, cs=comb_store):
                    if cs:
                        sy.wait_ge(cS.sem, cs)
                    sy.dma_start(out=xt[par][:],
                                 in_=x_h[b, tg * 128:(tg + 1) * 128, :]).then_inc(cL.sem, 16)
                @block.gpsimd
                def _(gq, b=b, tg=tg, sd=sil_done):
                    if sd:
                        gq.wait_ge(cV.sem, sd)
                    gq.dma_start(out=stok[:].rearrange("p (c t) -> p c t", t=128),
                                 in_=sT_h[b, tg].rearrange("c p t -> p c t")).then_inc(cW.sem, 16)
                cL.v += 16
                cW.v += 16
                ld_here = (cL.v, cW.v)

                cA.v += 1
                a_sil = cA.v
                @block.scalar
                def _(sc, tc=tc, ev=expert_evacs_done, sd=sil_done):
                    sc.wait_ge(cV.sem, ev)
                    if sd:
                        sc.wait_ge(cV.sem, sd)
                    sc.activation(g_sb[:], acc[:, tc * D:(tc + 1) * D], AF.Silu).then_inc(cA.sem, 1)

                @block.vector
                def _(v, par=par, a_sil=a_sil, ld=ld_here):
                    v.wait_ge(cA.sem, a_sil)
                    v.wait_ge(cL.sem, ld[0])
                    v.wait_ge(cW.sem, ld[1])
                    cV.inc(v.scalar_tensor_tensor(prod[:], stok[:], 1.0, g_sb[:],
                                                  op0=ALU.mult, op1=ALU.mult))
                    v.wait_ge(cV.sem, cV.v)
                    cV.inc(v.tensor_tensor(xt[par][:], xt[par][:], prod[:], op=ALU.add))
                sil_done = cV.v

                @block.sync
                def _(sy, b=b, tg=tg, par=par, sd=sil_done):
                    sy.wait_ge(cV.sem, sd)
                    sy.dma_start(out=x1_h[b, tg * 128:(tg + 1) * 128, :],
                                 in_=xt[par][:]).then_inc(cS.sem, 16)
                cS.v += 16
                if tc >= 1:
                    comb_store = cS.v - 16
                else:
                    comb_store = 0
            last_sil = cA.v

        @block.sync
        def _(sy):
            sy.wait_ge(cS.sem, cS.v)


def emit_l1_ln(ctx, x_h, ident, xtok_h, xnT_h):
    nc = ctx.nc
    PFX = 'e_'
    with contextlib.ExitStack() as ex:
        E = ex.enter_context
        NP = 4
        xt = [E(nc.sbuf_tensor(PFX + f"xt{p}", [128, D], F32)) for p in range(NP)]
        scr = E(nc.sbuf_tensor(PFX + "scr", [128, D], F32))
        xnb = [E(nc.sbuf_tensor(PFX + f"xnb{p}", [128, D], BF16)) for p in range(NP)]
        xnT_sb = E(nc.sbuf_tensor(PFX + "xnT_sb", [128, NC * S], BF16))
        id_sb = E(nc.sbuf_tensor(PFX + "id_sb", [128, 128], BF16))
        mv = E(nc.sbuf_tensor(PFX + "mv", [128, 4 * NP], F32))
        lnv = E(nc.sbuf_tensor(PFX + "lnv", [128, NP], F32))
        rstd = E(nc.sbuf_tensor(PFX + "rstd", [128, NP], F32))
        pb = [E(nc.psum_tensor(PFX + f"pb{k}", [128, 1024], BF16)) for k in range(2)]
        cL = ctx.cL
        cV = ctx.cV
        cA = ctx.cA
        cP = ctx.cP
        cS = ctx.cS
        block = ctx.block

        @block.sync
        def _(sy):
            sy.dma_start(out=id_sb[:], in_=ident[:, :]).then_inc(cL.sem, 16)
        cL.v += 16

        v_xn_hist = {}
        ev_bank = {0: 0, 1: 0}
        tr_par = {p: 0 for p in range(NP)}
        prev_store = {p: 0 for p in range(NP)}
        for b in range(B2):
            for i in range(NT):
                it = b * NT + i
                par = it % NP
                o = par * 4
                @block.sync
                def _(sy, b=b, i=i, par=par, it=it):
                    if it >= NP:
                        sy.wait_ge(cV.sem, v_xn_hist[it - NP])
                    sy.dma_start(out=xt[par][:],
                                 in_=x_h[b, i * 128:(i + 1) * 128, :]).then_inc(cL.sem, 16)
                cL.v += 16
                ld_here = cL.v

                cA.v += 1
                a_st = cA.v
                @block.scalar
                def _(sc, par=par, o=o, ld=ld_here, pvx=v_xn_hist.get(it - NP, 0)):
                    sc.wait_ge(cL.sem, ld)
                    if pvx:
                        sc.wait_ge(cV.sem, pvx)
                    sc.activation(scr[:], xt[par][:], AF.Identity, accum_out=mv[:, o:o + 1])
                    sc.activation(scr[:], xt[par][:], AF.Square,
                                  accum_out=mv[:, o + 3:o + 4]).then_inc(cA.sem, 1)

                @block.vector
                def _(v, a_st=a_st, o=o):
                    v.wait_ge(cA.sem, a_st)
                    v.tensor_scalar(mv[:, o:o + 1], mv[:, o:o + 1], 1.0 / D, None, op0=ALU.mult)
                    cV.inc(v.tensor_scalar(mv[:, o + 3:o + 4], mv[:, o + 3:o + 4], 1.0 / D, EPS,
                                           op0=ALU.mult, op1=ALU.add))
                    v.wait_ge(cV.sem, cV.v)
                    cV.inc(v.tensor_tensor(mv[:, o + 2:o + 3], mv[:, o:o + 1], mv[:, o:o + 1],
                                           op=ALU.mult))
                    v.wait_ge(cV.sem, cV.v)
                    cV.inc(v.tensor_tensor(mv[:, o + 1:o + 2], mv[:, o + 3:o + 4],
                                           mv[:, o + 2:o + 3], op=ALU.subtract))
                v_bn = cV.v

                cA.v += 2
                a_rstd = cA.v
                @block.scalar
                def _(sc, v_bn=v_bn, a1=a_rstd - 1, par=par, o=o):
                    sc.wait_ge(cV.sem, v_bn)
                    sc.activation(lnv[:, par:par + 1], mv[:, o + 1:o + 2], AF.Ln).then_inc(cA.sem, 1)
                    sc.wait_ge(cA.sem, a1)
                    sc.activation(rstd[:, par:par + 1], lnv[:, par:par + 1], AF.Exp,
                                  scale=-0.5).then_inc(cA.sem, 1)

                @block.vector
                def _(v, par=par, o=o, a_rstd=a_rstd, st=prev_store[par], tp=tr_par[par]):
                    v.wait_ge(cA.sem, a_rstd)
                    if st:
                        v.wait_ge(cS.sem, st)
                    if tp:
                        v.wait_ge(cP.sem, tp)
                    cV.inc(v.tensor_scalar(xnb[par][:], xt[par][:], mv[:, o:o + 1],
                                           rstd[:, par:par + 1],
                                           op0=ALU.subtract, op1=ALU.mult))
                v_xn_hist[it] = cV.v

                @block.sync
                def _(sy, b=b, i=i, par=par, vx=cV.v):
                    sy.wait_ge(cV.sem, vx)
                    sy.dma_start(out=xtok_h[b, i * 128:(i + 1) * 128, :],
                                 in_=xnb[par][:]).then_inc(cS.sem, 16)
                cS.v += 16
                prev_store[par] = cS.v

                for half in range(2):
                    bk = half
                    @block.tensor
                    def _(t, par=par, half=half, bk=bk, vx=v_xn_hist[it], ef=ev_bank[bk], it=it):
                        t.wait_ge(cV.sem, vx)
                        if ef:
                            t.wait_ge(cV.sem, ef)
                        if it == 0 and half == 0:
                            t.wait_ge(cL.sem, 16)
                        for s_idx in range(8):
                            dc = half * 8 + s_idx
                            last = t.transpose(pb[bk][:, s_idx * 128:(s_idx + 1) * 128],
                                               xnb[par][0:128, dc * 128:(dc + 1) * 128], id_sb[:])
                        cP.inc(last)
                    tr = cP.v

                    @block.vector
                    def _(v, i=i, half=half, bk=bk, tr=tr):
                        v.wait_ge(cP.sem, tr)
                        cV.inc(v.tensor_copy(
                            xnT_sb[:].rearrange("p (c t) -> p c t", t=S)[:, half * 8:half * 8 + 8,
                                                                         i * 128:(i + 1) * 128],
                            pb[bk][:].rearrange("p (c t) -> p c t", t=128)))
                    ev_bank[bk] = cV.v
                tr_par[par] = cP.v

            last_ev = cV.v
            @block.sync
            def _(sy, b=b, last_ev=last_ev):
                sy.wait_ge(cV.sem, last_ev)
                for dc in range(NC):
                    sy.dma_start(out=xnT_h[b, dc * 128:(dc + 1) * 128, :],
                                 in_=xnT_sb[:, dc * S:(dc + 1) * S]).then_inc(cS.sem, 16)
            cS.v += 16 * NC
            if b + 1 < B2:
                st_b = cS.v
                @block.tensor
                def _(t, st_b=st_b):
                    t.wait_ge(cS.sem, st_b)

        @block.sync
        def _(sy):
            sy.wait_ge(cS.sem, cS.v)


def make_wprep_stepper(ctx, E, id_sb, id_ld, wg_h, wu_h, wd_h, wgT, wuT, wdT):
    nc = ctx.nc
    PFX = 'f_'
    w_sb = [E(nc.sbuf_tensor(PFX + f"w_sb{p}", [128, FI], BF16)) for p in range(2)]
    stg = [E(nc.sbuf_tensor(PFX + f"stg{p}", [128, FI], BF16)) for p in range(2)]
    pb = [E(nc.psum_tensor(PFX + f"pb{k}", [128, 1024], BF16)) for k in range(2)]
    cW, cV, cP, cS = ctx.cW, ctx.cV, ctx.cP, ctx.cS
    block = ctx.block

    gu, dd = [], []
    for li in range(NI):
        gu.append(("g", wg_h, li, D // 128, wgT))
        gu.append(("u", wu_h, li, D // 128, wuT))
    for lh in range(NC):
        dd.append(("d", wd_h, lh, FI // 128, wdT))
    jobs = []
    for i in range(NC):
        jobs.extend(gu[i * 8:(i + 1) * 8])
        jobs.append(dd[i])

    state = {"jn": 0, "mm_free": {0: 0, 1: 0}, "ev_free": {0: 0, 1: 0},
             "st_free": {0: 0, 1: 0}}

    def emit_next(k):
        mm_free, ev_free, st_free = state["mm_free"], state["ev_free"], state["st_free"]
        for _ in range(k):
            jn = state["jn"]
            if jn >= len(jobs):
                return
            state["jn"] = jn + 1
            kind, src_h, loc, ncol, dst_h = jobs[jn]
            par = jn % 2
            W = ncol * 128

            @block.gpsimd
            def _(g, src_h=src_h, loc=loc, W=W, par=par, mf=mm_free[par]):
                if mf:
                    g.wait_ge(cP.sem, mf)
                g.dma_start(out=w_sb[par][:, 0:W],
                            in_=src_h[loc * 128:(loc + 1) * 128, 0:W]).then_inc(cW.sem, 16)
            cW.v += 16
            wl = cW.v

            for grp in range(ncol // 8):
                bk = grp % 2
                @block.tensor
                def _(t, par=par, grp=grp, bk=bk, wl=wl, ef=ev_free[bk],
                      first=(jn == 0 and grp == 0)):
                    t.wait_ge(cW.sem, wl)
                    if first:
                        t.wait_ge(cL_id[0].sem, cL_id[1])
                    if ef:
                        t.wait_ge(cV.sem, ef)
                    for s_idx in range(8):
                        cc = grp * 8 + s_idx
                        last = t.transpose(pb[bk][:, s_idx * 128:(s_idx + 1) * 128],
                                           w_sb[par][0:128, cc * 128:(cc + 1) * 128], id_sb[:])
                    cP.inc(last)
                tr = cP.v
                if grp == ncol // 8 - 1:
                    mm_free[par] = tr

                @block.vector
                def _(v, par=par, grp=grp, bk=bk, tr=tr, sf=st_free[par]):
                    v.wait_ge(cP.sem, tr)
                    if sf and grp < 2:
                        v.wait_ge(cS.sem, sf)
                    cV.inc(v.tensor_scalar(stg[par][:, grp * 1024:(grp + 1) * 1024],
                                           pb[bk][:], 128.0, None, op0=ALU.subtract))
                ev_free[bk] = cV.v
            ev_all = cV.v

            @block.sync
            def _(sy, dst_h=dst_h, loc=loc, ncol=ncol, par=par, ev_all=ev_all):
                sy.wait_ge(cV.sem, ev_all)
                sy.dma_start(out=dst_h[loc].rearrange("p c i -> p (c i)"),
                             in_=stg[par][:, 0:ncol * 128]).then_inc(cS.sem, 16)
            cS.v += 16
            st_free[par] = cS.v

    cL_id = (ctx.cL, id_ld)
    return emit_next


def emit_l1_attn(ctx, xtok_h, ident, cmask_h, s1_h):
    nc = ctx.nc
    PFX = 'g_'
    RSC = float(HD ** -0.25)
    with contextlib.ExitStack() as ex:
        E = ex.enter_context
        xt_sb = E(nc.sbuf_tensor(PFX + "xt_sb", [128, NT * D], BF16))
        stok = E(nc.sbuf_tensor(PFX + "stok", [128, NT * D], BF16))
        kT = [E(nc.sbuf_tensor(PFX + f"kT{s}", [128, S], BF16)) for s in range(2)]
        p_sb = [[E(nc.sbuf_tensor(PFX + f"p_sb{s}{p}", [128, S], BF16)) for p in range(2)]
                for s in range(2)]
        pT_sb = [E(nc.sbuf_tensor(PFX + f"pT_sb{s}", [128, NT * 128], BF16)) for s in range(2)]
        id_sb = E(nc.sbuf_tensor(PFX + "id_sb", [128, 128], BF16))
        cm_sb = E(nc.sbuf_tensor(PFX + "cm_sb", [128, 128], F32))
        nmax = E(nc.sbuf_tensor(PFX + "nmax", [128, 4], F32))
        rsum = E(nc.sbuf_tensor(PFX + "rsum", [128, 4], F32))
        rr = E(nc.sbuf_tensor(PFX + "rr", [128, 4], F32))
        pS = E(nc.psum_tensor(PFX + "pS", [128, 2048], F32))
        pT0 = E(nc.psum_tensor(PFX + "pT0", [128, 1024], BF16))
        pT1 = E(nc.psum_tensor(PFX + "pT1", [128, 1024], BF16))
        pO = [E(nc.psum_tensor(PFX + f"pO{s}", [128, 512], F32)) for s in range(2)]
        cL, cW, cV, cA, cP, cS = ctx.cL, ctx.cW, ctx.cV, ctx.cA, ctx.cP, ctx.cS
        block = ctx.block
        pT = [pT0, pT1]

        @block.sync
        def _(sy):
            sy.dma_start(out=id_sb[:], in_=ident[:, :]).then_inc(cL.sem, 16)
            sy.dma_start(out=cm_sb[:], in_=cmask_h[:, :]).then_inc(cL.sem, 16)
        cL.v += 32

        bank_free = {0: None, 1: None}
        bkc = [0]
        kT_last_use = [0, 0]
        last_pS_exp = [0]
        for b in range(B2):
            @block.sync
            def _(sy, b=b, pdone=cS.v):
                if b > 0:
                    sy.wait_ge(cS.sem, pdone)
                for tc in range(NT):
                    sy.dma_start(out=xt_sb[:, tc * D:(tc + 1) * D],
                                 in_=xtok_h[b, tc * 128:(tc + 1) * 128, :]).then_inc(cL.sem, 16)
            cL.v += 16 * NT
            ld_x = cL.v

            for hp in range(H // 2):
                heads = [hp * 2, hp * 2 + 1]
                kT_ready = [0, 0]
                for s in range(2):
                    h = heads[s]
                    for half in range(2):
                        bk = bkc[0]; bkc[0] ^= 1
                        @block.tensor
                        def _(t, h=h, s=s, half=half, bk=bk, ld_x=ld_x,
                              bf=bank_free[bk], ku=kT_last_use[s]):
                            t.wait_ge(cL.sem, ld_x)
                            if half == 0 and ku:
                                t.wait_ge(cP.sem, ku)
                            if bf is not None:
                                t.wait_ge(bf[0].sem, bf[1])
                            for s_idx in range(8):
                                tc = half * 8 + s_idx
                                last = t.transpose(pT[bk][:, s_idx * 128:(s_idx + 1) * 128],
                                                   xt_sb[0:128, tc * D + h * 128: tc * D + (h + 1) * 128],
                                                   id_sb[:])
                            cP.inc(last)
                        tr = cP.v

                        @block.vector
                        def _(v, s=s, half=half, bk=bk, tr=tr):
                            v.wait_ge(cP.sem, tr)
                            cV.inc(v.tensor_scalar(
                                kT[s][:, half * 1024:(half + 1) * 1024], pT[bk][:], RSC,
                                None, op0=ALU.mult))
                        bank_free[bk] = (cV, cV.v)
                    kT_ready[s] = cV.v

                exp_hist = [{}, {}]
                ptg_hist = [{}, {}]
                rr_hist = [{}, {}]
                o_evac_done = [0, 0]

                def emit_softmax(s, qi):
                    qp = qi % 2
                    col = s * 2 + qp
                    kw = (qi + 1) * 128
                    nblk = (kw + 511) // 512
                    @block.tensor
                    def _(t, s=s, qi=qi, kw=kw, nblk=nblk, ed=last_pS_exp[0],
                          kr=kT_ready[s]):
                        t.wait_ge(cV.sem, kr)
                        if ed:
                            t.wait_ge(cA.sem, ed)
                        for n in range(nblk):
                            w = min(512, kw - n * 512)
                            last = t.matmul(pS[:, n * 512:n * 512 + w],
                                            kT[s][:, qi * 128:(qi + 1) * 128],
                                            kT[s][:, n * 512:n * 512 + w], start=True, stop=True)
                        cP.inc(last)
                    s_done = cP.v
                    kT_last_use[s] = cP.v

                    @block.vector
                    def _(v, qi=qi, col=col, kw=kw, s_done=s_done):
                        v.wait_ge(cP.sem, s_done)
                        cV.inc(v.tensor_tensor(pS[:, qi * 128:(qi + 1) * 128],
                                               pS[:, qi * 128:(qi + 1) * 128], cm_sb[:], op=ALU.add))
                        v.wait_ge(cV.sem, cV.v)
                        cV.inc(v.tensor_reduce(nmax[:, col:col + 1], pS[:, 0:kw], axis=AX.X,
                                               op=ALU.max, negate=True))
                    max_done = cV.v

                    cA.v += 1
                    exp_hist[s][qi] = cA.v
                    last_pS_exp[0] = cA.v
                    @block.scalar
                    def _(sc, s=s, qi=qi, qp=qp, col=col, kw=kw, max_done=max_done,
                          ptg=ptg_hist[s].get(qi - 2, 0)):
                        sc.wait_ge(cV.sem, max_done)
                        if ptg:
                            sc.wait_ge(cP.sem, ptg)
                        sc.activation(p_sb[s][qp][:, 0:kw], pS[:, 0:kw], AF.Exp,
                                      bias=nmax[:, col:col + 1],
                                      accum_out=rsum[:, col:col + 1]).then_inc(cA.sem, 1)

                    @block.vector
                    def _(v, col=col, ed=exp_hist[s][qi]):
                        v.wait_ge(cA.sem, ed)
                        cV.inc(v.reciprocal(rr[:, col:col + 1], rsum[:, col:col + 1]))
                        v.wait_ge(cV.sem, cV.v)
                    rr_hist[s][qi] = cV.v

                def emit_pv(s, qi):
                    qp = qi % 2
                    col = s * 2 + qp
                    h = heads[s]
                    ntile = qi + 1
                    first_grp = True
                    for base in range(0, ntile, 8):
                        cnt = min(8, ntile - base)
                        bk = bkc[0]; bkc[0] ^= 1
                        @block.tensor
                        def _(t, s=s, qp=qp, base=base, cnt=cnt, bk=bk, bf=bank_free[bk],
                              fg=first_grp, ed=exp_hist[s][qi]):
                            if fg:
                                t.wait_ge(cA.sem, ed)
                            if bf is not None:
                                t.wait_ge(bf[0].sem, bf[1])
                            for s_idx in range(cnt):
                                kc = base + s_idx
                                last = t.transpose(pT[bk][:, s_idx * 128:(s_idx + 1) * 128],
                                                   p_sb[s][qp][0:128, kc * 128:(kc + 1) * 128],
                                                   id_sb[:])
                            cP.inc(last)
                        first_grp = False
                        tr = cP.v

                        @block.vector
                        def _(v, s=s, base=base, cnt=cnt, bk=bk, tr=tr):
                            v.wait_ge(cP.sem, tr)
                            cV.inc(v.tensor_copy(
                                pT_sb[s][:].rearrange("p (c t) -> p c t", t=128)[:, base:base + cnt, :],
                                pT[bk][:, 0:cnt * 128].rearrange("p (c t) -> p c t", t=128)))
                        bank_free[bk] = (cV, cV.v)
                    pt_ready = cV.v
                    ptg_hist[s][qi] = cP.v

                    @block.tensor
                    def _(t, s=s, qi=qi, h=h, pt_ready=pt_ready, oed=o_evac_done[s]):
                        t.wait_ge(cV.sem, pt_ready)
                        if oed:
                            t.wait_ge(cA.sem, oed)
                        for kc in range(qi + 1):
                            last = t.matmul(pO[s][:, 0:128], pT_sb[s][:, kc * 128:(kc + 1) * 128],
                                            xt_sb[:, kc * D + h * 128: kc * D + (h + 1) * 128],
                                            start=(kc == 0), stop=(kc == qi))
                        cP.inc(last)
                    pv_done = cP.v

                    cA.v += 1
                    @block.scalar
                    def _(sc, s=s, qi=qi, col=col, h=h, pv_done=pv_done, rrd=rr_hist[s][qi]):
                        sc.wait_ge(cP.sem, pv_done)
                        sc.wait_ge(cV.sem, rrd)
                        sc.activation(stok[:, qi * D + h * 128: qi * D + (h + 1) * 128],
                                      pO[s][:, 0:128], AF.Identity,
                                      scale=rr[:, col:col + 1]).then_inc(cA.sem, 1)
                    o_evac_done[s] = cA.v

                for qi in range(NT + 1):
                    for s in range(2):
                        if qi >= 1:
                            emit_pv(s, qi - 1)
                        if qi < NT:
                            emit_softmax(s, qi)

            oe = cA.v
            @block.sync
            def _(sy, b=b, oe=oe):
                sy.wait_ge(cA.sem, oe)
                for tc in range(NT):
                    sy.dma_start(out=s1_h[b, tc * 128:(tc + 1) * 128, :],
                                 in_=stok[:, tc * D:(tc + 1) * D]).then_inc(cS.sem, 16)
            cS.v += 16 * NT

        @block.sync
        def _(sy):
            sy.wait_ge(cS.sem, cS.v)

def emit_l1_ffn(ctx, x_h, xnT_h, s1_h, wgT_h, wuT_h, wdT_h, gsc_h, usc_h, dsc_h, ident, out_h):
    nc = ctx.nc
    PFX = 'h_'
    with contextlib.ExitStack() as ex:
        E = ex.enter_context
        id_sb = E(nc.sbuf_tensor(PFX + "id_sb", [128, 128], BF16))
        xb_sb = [E(nc.sbuf_tensor(PFX + f"xb_sb{p}", [128, NC * 512], BF16)) for p in range(2)]
        ht = E(nc.sbuf_tensor(PFX + "ht", [128, NI * 512], BF16))
        wg_sb = [E(nc.sbuf_tensor(PFX + f"wg_sb{p}", [128, NC * 128], BF16)) for p in range(2)]
        wu_sb = [E(nc.sbuf_tensor(PFX + f"wu_sb{p}", [128, NC * 128], BF16)) for p in range(2)]
        wd_sb = [E(nc.sbuf_tensor(PFX + f"wd_sb{p}", [128, 32 * 128], BF16)) for p in range(4)]
        sg_sb = [E(nc.sbuf_tensor(PFX + f"sg_sb{p}", [128, 512], BF16)) for p in range(2)]
        gts_sb = [E(nc.sbuf_tensor(PFX + f"gts_sb{p}", [128, 512], BF16)) for p in range(2)]
        gtok = E(nc.sbuf_tensor(PFX + "gtok", [128, 4 * D], BF16))
        s1r = [E(nc.sbuf_tensor(PFX + f"s1r{p}", [128, D], BF16)) for p in range(2)]
        xt = [E(nc.sbuf_tensor(PFX + f"xtc{p}", [128, D], F32)) for p in range(2)]
        prod = E(nc.sbuf_tensor(PFX + "prod", [128, D], F32))
        gsc = E(nc.sbuf_tensor(PFX + "gsc_sb", [128, NI], F32))
        usc = E(nc.sbuf_tensor(PFX + "usc_sb", [128, NI], F32))
        dsc = E(nc.sbuf_tensor(PFX + "dsc_sb", [128, NC], F32))
        pG = [E(nc.psum_tensor(PFX + f"pG{p}", [128, 512], F32)) for p in range(2)]
        pU = [E(nc.psum_tensor(PFX + f"pU{p}", [128, 512], F32)) for p in range(2)]
        pD = [E(nc.psum_tensor(PFX + f"pD{p}", [128, 512], F32)) for p in range(2)]
        pTT = [E(nc.psum_tensor(PFX + f"pTT{p}", [128, 1024], BF16)) for p in range(2)]
        cL = ctx.cL
        cW = ctx.cW
        cV = ctx.cV
        cA = ctx.cA
        cP = ctx.cP
        cS = ctx.cS
        block = ctx.block

        @block.sync
        def _(sy):
            sy.dma_start(out=id_sb[:], in_=ident[:, :]).then_inc(cL.sem, 16)
            sy.dma_start(out=gsc[:], in_=gsc_h[:, :]).then_inc(cL.sem, 16)
            sy.dma_start(out=usc[:], in_=usc_h[:, :]).then_inc(cL.sem, 16)
            sy.dma_start(out=dsc[:], in_=dsc_h[:, :]).then_inc(cL.sem, 16)
        cL.v += 64

        wld = {}
        wdld = {}
        gu_free = {0: (None, 0), 1: (None, 0)}
        st_free = {0: 0, 1: 0}
        lastgu = {}
        xb_ld = {}
        last_down_prev_tb = 0

        def emit_xb_load(tbi):
            b_, tb_ = divmod(tbi, 4)
            par = tbi % 2
            gate = lastgu.get(tbi - 2, 0)
            @block.sync
            def _(sy, b=b_, tb=tb_, par=par, gate=gate):
                if gate:
                    sy.wait_ge(cP.sem, gate)
                for hc in range(NC):
                    sy.dma_start(out=xb_sb[par][:, hc * 512:(hc + 1) * 512],
                                 in_=xnT_h[b, hc * 128:(hc + 1) * 128,
                                           tb * 512:(tb + 1) * 512]).then_inc(cL.sem, 16)
            cL.v += 16 * NC
            xb_ld[tbi] = cL.v

        emit_xb_load(0)
        for b in range(B2):
            for tb in range(4):
                tbi = b * 4 + tb
                xpar = tbi % 2

                silu_done = {}
                ht_done = 0
                for ic in range(NI):
                    par = ic % 2
                    @block.gpsimd
                    def _(g, ic=ic, par=par, gf=gu_free[par]):
                        if gf[0] is not None:
                            g.wait_ge(cP.sem, gf[1])
                        g.dma_start(out=wg_sb[par][:], in_=wgT_h[ic].rearrange("p c i -> p (c i)")
                                    ).then_inc(cW.sem, 16)
                        g.dma_start(out=wu_sb[par][:], in_=wuT_h[ic].rearrange("p c i -> p (c i)")
                                    ).then_inc(cW.sem, 16)
                    cW.v += 32
                    wld[ic] = cW.v

                    @block.tensor
                    def _(t, ic=ic, par=par, w_ld=wld[ic], xld=xb_ld[tbi], xpar=xpar,
                          sfree=silu_done.get(ic - 2, 0), ufree=0):
                        t.wait_ge(cW.sem, w_ld)
                        t.wait_ge(cL.sem, xld)
                        if sfree:
                            t.wait_ge(cV.sem, sfree)
                        for hc in range(NC):
                            lastg = t.matmul(pG[par][:], wg_sb[par][:, hc * 128:(hc + 1) * 128],
                                             xb_sb[xpar][:, hc * 512:(hc + 1) * 512],
                                             start=(hc == 0), stop=(hc == NC - 1))
                        cP.inc(lastg)
                        for hc in range(NC):
                            lastu = t.matmul(pU[par][:], wu_sb[par][:, hc * 128:(hc + 1) * 128],
                                             xb_sb[xpar][:, hc * 512:(hc + 1) * 512],
                                             start=(hc == 0), stop=(hc == NC - 1))
                        cP.inc(lastu)
                    gu_free[par] = (cP, cP.v)
                    g_mm = cP.v - 1
                    u_mm = cP.v

                    cA.v += 1
                    a_sil = cA.v
                    @block.scalar
                    def _(sc, ic=ic, par=par, g_mm=g_mm, sfree=silu_done.get(ic - 2, 0)):
                        sc.wait_ge(cP.sem, g_mm)
                        if sfree:
                            sc.wait_ge(cV.sem, sfree)
                        sc.activation(sg_sb[par][:], pG[par][:], AF.Silu,
                                      scale=gsc[:, ic:ic + 1]).then_inc(cA.sem, 1)

                    @block.vector
                    def _(v, ic=ic, par=par, u_mm=u_mm, a_sil=a_sil,
                          hfree=(last_down_prev_tb if ic == 0 else 0)):
                        v.wait_ge(cP.sem, u_mm)
                        v.wait_ge(cA.sem, a_sil)
                        if hfree:
                            v.wait_ge(cP.sem, hfree)
                        cV.inc(v.scalar_tensor_tensor(ht[:, ic * 512:(ic + 1) * 512],
                                                      pU[par][:], usc[:, ic:ic + 1], sg_sb[par][:],
                                                      op0=ALU.mult, op1=ALU.mult))
                    silu_done[ic] = cV.v
                lastgu[tbi] = cP.v
                if tbi + 1 < B2 * 4:
                    emit_xb_load(tbi + 1)
                ht_ready = cV.v

                gt_tr = {}
                ev_done = {0: 0, 1: 0}
                dn_free = {0: (None, 0), 1: (None, 0), 2: (None, 0), 3: (None, 0)}
                for hc in range(NC):
                    for half in range(2):
                        par = (hc * 2 + half) % 4
                        @block.sync
                        def _(sy, hc=hc, half=half, par=par, df=dn_free[par]):
                            if df[0] is not None:
                                sy.wait_ge(cP.sem, df[1])
                            sy.dma_start(out=wd_sb[par][:],
                                         in_=wdT_h[hc, :, half * 32:(half + 1) * 32, :]
                                         .rearrange("p c i -> p (c i)")).then_inc(cL.sem, 16)
                        cL.v += 16
                        wdld[(hc, half)] = cL.v

                    pdp = hc % 2
                    @block.tensor
                    def _(t, hc=hc, pdp=pdp, ht_ready=ht_ready,
                          sfree=gt_tr.get(hc - 2, 0)):
                        t.wait_ge(cV.sem, ht_ready)
                        if sfree:
                            t.wait_ge(cP.sem, sfree)
                        for half in range(2):
                            par = (hc * 2 + half) % 4
                            t.wait_ge(cL.sem, wdld[(hc, half)])
                            for k in range(32):
                                ic = half * 32 + k
                                last = t.matmul(pD[pdp][:], wd_sb[par][:, k * 128:(k + 1) * 128],
                                                ht[:, ic * 512:(ic + 1) * 512],
                                                start=(ic == 0), stop=(ic == NI - 1))
                        cP.inc(last)
                    dn_free[(hc * 2) % 4] = (cP, cP.v)
                    dn_free[(hc * 2 + 1) % 4] = (cP, cP.v)
                    d_mm = cP.v

                    gpar = hc % 2
                    cA.v += 1
                    a_gt = cA.v
                    @block.scalar
                    def _(sc, hc=hc, pdp=pdp, gpar=gpar, d_mm=d_mm, tfree=gt_tr.get(hc - 2, 0)):
                        sc.wait_ge(cP.sem, d_mm)
                        if tfree:
                            sc.wait_ge(cP.sem, tfree)
                        sc.activation(gts_sb[gpar][:], pD[pdp][:], AF.Silu,
                                      scale=dsc[:, hc:hc + 1]).then_inc(cA.sem, 1)

                    bk = hc % 2
                    @block.tensor
                    def _(t, hc=hc, gpar=gpar, bk=bk, a_gt=a_gt, ef=ev_done[bk]):
                        t.wait_ge(cA.sem, a_gt)
                        if ef:
                            t.wait_ge(cV.sem, ef)
                        for i in range(4):
                            last = t.transpose(pTT[bk][:, i * 128:(i + 1) * 128],
                                               gts_sb[gpar][0:128, i * 128:(i + 1) * 128], id_sb[:])
                        cP.inc(last)
                    gt_tr[hc] = cP.v
                    tr = cP.v

                    @block.vector
                    def _(v, hc=hc, bk=bk, tr=tr, cfree=(0 if tbi == 0 else comb_done)):
                        v.wait_ge(cP.sem, tr)
                        if hc == 0 and cfree:
                            v.wait_ge(cV.sem, cfree)
                        cV.inc(v.tensor_copy(
                            gtok[:].rearrange("p (s d) -> p s d", d=D)[:, :, hc * 128:(hc + 1) * 128],
                            pTT[bk][:, 0:512].rearrange("p (s t) -> p s t", t=128)))
                    ev_done[bk] = cV.v
                last_down_prev_tb = cP.v
                g_ready = cV.v

                for tsub in range(4):
                    tg = tb * 4 + tsub
                    par = tsub % 2
                    @block.sync
                    def _(sy, b=b, tg=tg, par=par, sf=st_free[par]):
                        if sf:
                            sy.wait_ge(cS.sem, sf)
                        sy.dma_start(out=xt[par][:],
                                     in_=x_h[b, tg * 128:(tg + 1) * 128, :]).then_inc(cL.sem, 16)
                        sy.dma_start(out=s1r[par][:],
                                     in_=s1_h[b, tg * 128:(tg + 1) * 128, :]).then_inc(cL.sem, 16)
                    cL.v += 32
                    c_ld = cL.v

                    @block.vector
                    def _(v, tsub=tsub, par=par, c_ld=c_ld, g_ready=g_ready):
                        v.wait_ge(cL.sem, c_ld)
                        v.wait_ge(cV.sem, g_ready)
                        cV.inc(v.scalar_tensor_tensor(prod[:], s1r[par][:], 1.0,
                                                      gtok[:, tsub * D:(tsub + 1) * D],
                                                      op0=ALU.mult, op1=ALU.mult))
                        v.wait_ge(cV.sem, cV.v)
                        cV.inc(v.tensor_tensor(xt[par][:], xt[par][:], prod[:], op=ALU.add))
                    c_done = cV.v

                    @block.sync
                    def _(sy, b=b, tg=tg, par=par, c_done=c_done):
                        sy.wait_ge(cV.sem, c_done)
                        sy.dma_start(out=out_h[b, tg * 128:(tg + 1) * 128, :],
                                     in_=xt[par][:]).then_inc(cS.sem, 16)
                    cS.v += 16
                    st_free[par] = cS.v
                comb_done = cV.v

        @block.sync
        def _(sy):
            sy.wait_ge(cS.sem, cS.v)


class DeferBlock:
    def __init__(self, real):
        self.real = real
        self.squeue = []

    def scalar(self, f):
        self.squeue.append(f)

    def flush(self):
        if self.squeue:
            fns = self.squeue
            self.squeue = []
            def runner(sc, fns=fns):
                for f in fns:
                    f(sc)
            self.real.scalar(runner)

    def tensor(self, f):
        self.real.tensor(f)

    def vector(self, f):
        self.real.vector(f)

    def sync(self, f):
        self.real.sync(f)

    def gpsimd(self, f):
        self.real.gpsimd(f)


def barrier(ctx):
    ctx.block.flush()
    cs = [ctx.cL, ctx.cW, ctx.cV, ctx.cA, ctx.cP, ctx.cS]
    vals = [(c.sem, c.v) for c in cs]
    for engname in ("tensor", "vector", "scalar", "sync", "gpsimd"):
        def f(e, vals=vals):
            for s, v in vals:
                if v:
                    e.wait_ge(s, v)
        getattr(ctx.block.real, engname)(f)


class _Ctx:
    pass


def build_fused():
    nc = bass.Bass("TRN2", target_bir_lowering=False)
    x = nc.dram_tensor("x", [B2, S, D], F32, kind="ExternalInput").ap()
    wq_h = nc.dram_tensor("wq", [D, D], F32, kind="ExternalInput").ap()
    wk_h = nc.dram_tensor("wk", [D, D], F32, kind="ExternalInput").ap()
    wv_h = nc.dram_tensor("wv", [D, D], F32, kind="ExternalInput").ap()
    wo_h = nc.dram_tensor("wo", [D, D], F32, kind="ExternalInput").ap()
    gw = nc.dram_tensor("gw", [NE, D], F32, kind="ExternalInput").ap()
    expw_h = nc.dram_tensor("expw", [NE, D, D], F32, kind="ExternalInput").ap()
    expb_h = nc.dram_tensor("expb", [NE, D], F32, kind="ExternalInput").ap()
    wg_h = nc.dram_tensor("l1wg", [FI, D], I32, kind="ExternalInput").ap()
    wu_h = nc.dram_tensor("l1wu", [FI, D], I32, kind="ExternalInput").ap()
    wd_h = nc.dram_tensor("l1wd", [D, FI], I32, kind="ExternalInput").ap()
    gsc_h = nc.dram_tensor("gsc", [128, NI], F32, kind="ExternalInput").ap()
    usc_h = nc.dram_tensor("usc", [128, NI], F32, kind="ExternalInput").ap()
    dsc_h = nc.dram_tensor("dsc", [128, NC], F32, kind="ExternalInput").ap()
    ident = nc.dram_tensor("ident", [128, 128], BF16, kind="ExternalInput").ap()
    identf = nc.dram_tensor("identf", [128, 128], F32, kind="ExternalInput").ap()
    cmask_h = nc.dram_tensor("cmask", [128, 128], F32, kind="ExternalInput").ap()
    cmaskt_h = nc.dram_tensor("cmaskt", [128, 128], F32, kind="ExternalInput").ap()
    onesb = nc.dram_tensor("onesb", [1, 128], BF16, kind="ExternalInput").ap()
    out_h = nc.dram_tensor("out", [B2, S, D], F32, kind="ExternalOutput").ap()
    xnT = nc.dram_tensor("i_xnT", [B2, D, S], BF16, kind="Internal").ap()
    gates = nc.dram_tensor("i_gates", [B2, 128, NT * NE], F32, kind="Internal").ap()
    attnT = nc.dram_tensor("i_attnT", [B2, D, S], BF16, kind="Internal").ap()
    sT = nc.dram_tensor("i_sT", [B2, NT, NC, 128, 128], BF16, kind="Internal").ap()
    x1 = nc.dram_tensor("i_x1", [B2, S, D], F32, kind="Internal").ap()
    xtok1 = nc.dram_tensor("i_xtok1", [B2, S, D], BF16, kind="Internal").ap()
    xnT1 = nc.dram_tensor("i_xnT1", [B2, D, S], BF16, kind="Internal").ap()
    s1 = nc.dram_tensor("i_s1", [B2, S, D], BF16, kind="Internal").ap()
    wgT = nc.dram_tensor("i_wgT", [NI, 128, NC, 128], BF16, kind="Internal").ap()
    wuT = nc.dram_tensor("i_wuT", [NI, 128, NC, 128], BF16, kind="Internal").ap()
    wdT = nc.dram_tensor("i_wdT", [NC, 128, NI, 128], BF16, kind="Internal").ap()
    with contextlib.ExitStack() as gx:
        ctx = _Ctx()
        ctx.nc = nc
        for cn in ("cL", "cW", "cV", "cA", "cP", "cS"):
            setattr(ctx, cn, Ctr(gx, nc, cn))
        ctx.block = DeferBlock(gx.enter_context(nc.Block()))
        emit_l0_ln(ctx, x, gw, ident, xnT, gates,
                   wprep_args=(wg_h, wu_h, wd_h, wgT, wuT, wdT))
        barrier(ctx)
        emit_l0_attn(ctx, xnT, wq_h, wk_h, wv_h, ident, identf, attnT)
        barrier(ctx)
        emit_l0_wo(ctx, attnT, wo_h, ident, sT)
        barrier(ctx)
        emit_l0_moe(ctx, x, xnT, sT, gates, expw_h, expb_h, ident, onesb, x1)
        barrier(ctx)
        emit_l1_ln(ctx, x1, ident, xtok1, xnT1)
        barrier(ctx)
        emit_l1_attn(ctx, xtok1, ident, cmask_h, s1)
        barrier(ctx)
        emit_l1_ffn(ctx, x1, xnT1, s1, wgT, wuT, wdT, gsc_h, usc_h, dsc_h, ident, out_h)
        ctx.block.flush()
    return nc


_IDENT_BF = np.eye(128, dtype=BF)


def _shard_x(x):
    return [np.ascontiguousarray(x[c * B2:(c + 1) * B2]) for c in range(NCORE)]


def kernel(**inputs):
    nc = build_fused()
    shards = _shard_x(inputs["L0_attn_hidden_states"])
    cmask = np.triu(np.full((128, 128), -1e9, np.float32), 1)
    ones = np.ones((1, 128), dtype=BF)
    gsc = np.ascontiguousarray(inputs["L1_ffn_gate_scale"].reshape(NI, 128).T)
    usc = np.ascontiguousarray(inputs["L1_ffn_up_scale"].reshape(NI, 128).T)
    dsc = np.ascontiguousarray(inputs["L1_ffn_down_scale"].reshape(NC, 128).T)
    in_maps = [{
        "x": shards[c],
        "wq": inputs["L0_attn_Wq"], "wk": inputs["L0_attn_Wk"],
        "wv": inputs["L0_attn_Wv"], "wo": inputs["L0_attn_Wo"],
        "gw": inputs["L0_ffn_gate_w"],
        "expw": inputs["L0_ffn_expert_w"], "expb": inputs["L0_ffn_expert_b"],
        "l1wg": inputs["L1_ffn_gate_w"], "l1wu": inputs["L1_ffn_up_w"],
        "l1wd": inputs["L1_ffn_down_w"],
        "gsc": gsc, "usc": usc, "dsc": dsc,
        "ident": _IDENT_BF, "identf": np.eye(128, dtype=np.float32),
        "cmask": cmask, "cmaskt": np.ascontiguousarray(cmask.T), "onesb": ones,
    } for c in range(NCORE)]
    res = run_bass_kernel_spmd(nc, in_maps, core_ids=list(range(NCORE)))
    return np.concatenate([r["out"] for r in res.results], axis=0)
